# version 15 (final)
# baseline (speedup 1.0000x reference)
import numpy as np
import ml_dtypes
import concourse.bass as bass
import concourse.mybir as mybir
from concourse.bass_utils import run_bass_kernel_spmd
from concourse.alu_op_type import AluOpType as OP

F32 = mybir.dt.float32
BF16 = mybir.dt.bfloat16
AF = mybir.ActivationFunctionType
AX = mybir.AxisListType

DEBUG = False
DEBUG_XFM = False
WARMUP = False
KVW = 129
AMPUTATE = False
SNAPDUMP = False
N_CORES = 8
TC = 2048
NCH = 16
D = 1024
E = 16
INTER = 512
EPS_A = 1e-6
EPS_LN = 1e-5
BIG = 1.0e9


class Sched:

    ENGINES = ("tensor", "vector", "scalar", "gpsimd", "sync")

    def __init__(self, nc):
        self.nc = nc
        self.ops = {e: [] for e in self.ENGINES}
        self.sems = {}

    def sem(self, name, handle):
        self.sems[name] = [handle, 0]

    def emit(self, eng, fn, inc=None, k=1):
        if inc is not None:
            h = self.sems[inc][0]
            self.sems[inc][1] += k
            self.ops[eng].append(lambda e, fn=fn, h=h, k=k: fn(e).then_inc(h, k))
        else:
            self.ops[eng].append(fn)

    def wait(self, eng, sem, val=None):
        h, cnt = self.sems[sem]
        v = cnt if val is None else val
        if v and v > 0:
            self.ops[eng].append(lambda e, h=h, v=v: e.wait_ge(h, v))

    def count(self, sem):
        return self.sems[sem][1]

    def drain(self, eng):
        self.ops[eng].append(lambda e: e.drain())

    def run(self, block):
        for engname in self.ENGINES:
            def body(eng, engname=engname):
                for fn in self.ops[engname]:
                    fn(eng)
            getattr(block, engname)(body)


def build_nc():
    nc = bass.Bass("TRN2", target_bir_lowering=False, debug=False)

    xo = nc.dram_tensor("x_own", [TC, D], F32, kind="ExternalInput").ap()
    wq = nc.dram_tensor("wq", [D, D], F32, kind="ExternalInput").ap()
    wk = nc.dram_tensor("wk", [D, D], F32, kind="ExternalInput").ap()
    wv = nc.dram_tensor("wv", [D, D], F32, kind="ExternalInput").ap()
    pj = nc.dram_tensor("proj", [64, 64], F32, kind="ExternalInput").ap()
    wo = nc.dram_tensor("wo", [D, D], F32, kind="ExternalInput").ap()
    bo = nc.dram_tensor("bo", [1, D], F32, kind="ExternalInput").ap()
    gwt = nc.dram_tensor("gwt", [D, E], F32, kind="ExternalInput").ap()
    gbias = nc.dram_tensor("gbias", [128, E], F32, kind="ExternalInput").ap()
    identc = nc.dram_tensor("identc", [128, 128], BF16, kind="ExternalInput").ap()
    onehc = nc.dram_tensor("onehc", [16, 2048], BF16, kind="ExternalInput").ap()
    w1t = nc.dram_tensor("w1t", [E, D, INTER], F32, kind="ExternalInput").ap()
    w3t = nc.dram_tensor("w3t", [E, D, INTER], F32, kind="ExternalInput").ap()
    w2t = nc.dram_tensor("w2t", [E, INTER, D], F32, kind="ExternalInput").ap()
    ws1t = nc.dram_tensor("ws1t", [D, INTER], F32, kind="ExternalInput").ap()
    ws3t = nc.dram_tensor("ws3t", [D, INTER], F32, kind="ExternalInput").ap()
    ws2t = nc.dram_tensor("ws2t", [INTER, D], F32, kind="ExternalInput").ap()
    out = nc.dram_tensor("out", [TC, D], F32, kind="ExternalOutput").ap()
    kv_ci = nc.dram_tensor("kv_ci", [128, 1032], BF16, kind="Internal").ap()
    kv_co = nc.dram_tensor("kv_co", [128, 1032], BF16, kind="Internal", addr_space="Local").ap()
    dbg = {}
    if DEBUG:
        dbg["kvblk"] = nc.dram_tensor("d_kvblk", [128, 1024], BF16, kind="ExternalOutput").ap()
        dbg["ksel"] = nc.dram_tensor("d_ksel", [128, 145], BF16, kind="ExternalOutput").ap()
        dbg["kvstage"] = nc.dram_tensor("d_kvstage", [128, 1032], BF16, kind="ExternalOutput").ap()
        dbg["kvparts"] = nc.dram_tensor("d_kvparts", [128, 15 * 1032], BF16, kind="ExternalOutput").ap()
        dbg["kvp456"] = nc.dram_tensor("d_kvp456", [128, 1536], BF16, kind="ExternalOutput").ap()
        dbg["kvred"] = nc.dram_tensor("d_kvred", [128, 1536], BF16, kind="ExternalOutput").ap()
        dbg["outfm"] = nc.dram_tensor("d_outfm", [128, 16384], BF16, kind="ExternalOutput").ap()
        dbg["x1"] = nc.dram_tensor("d_x1", [TC, D], F32, kind="ExternalOutput").ap()
        dbg["scores"] = nc.dram_tensor("d_scores", [128, NCH * 16], F32, kind="ExternalOutput").ap()
        dbg["comb"] = nc.dram_tensor("d_comb", [128, NCH * 16], BF16, kind="ExternalOutput").ap()
        dbg["xfm"] = nc.dram_tensor("d_xfm", [128, 16384], BF16, kind="ExternalOutput").ap()
        dbg["kfm"] = nc.dram_tensor("d_kfm", [128, 4096], BF16, kind="ExternalOutput").ap()
        dbg["vext"] = nc.dram_tensor("d_vext", [128, 1056], BF16, kind="ExternalOutput").ap()
        dbg["kftm"] = nc.dram_tensor("d_kftm", [128, 1024], BF16, kind="ExternalOutput").ap()
        dbg["p4raw"] = nc.dram_tensor("d_p4raw", [128, 512], F32, kind="ExternalOutput").ap()

    from contextlib import ExitStack
    with ExitStack() as ctx:
        resid = ctx.enter_context(nc.sbuf_tensor("resid", [128, NCH * D], F32))
        xcast = ctx.enter_context(nc.sbuf_tensor("xcast", [128, 2 * D], BF16))
        xfm = ctx.enter_context(nc.sbuf_tensor("xfm", [128, 8 * TC], BF16))
        wA = ctx.enter_context(nc.sbuf_tensor("wA", [128, 12288], BF16))
        wB = ctx.enter_context(nc.sbuf_tensor("wB", [128, 12288], BF16))
        big0 = ctx.enter_context(nc.sbuf_tensor("big0", [128, 16384], BF16))
        vext = ctx.enter_context(nc.sbuf_tensor("vext", [128, 8 * 129], BF16))
        kftm = ctx.enter_context(nc.sbuf_tensor("kftm", [128, 1024], BF16))
        kfm = ctx.enter_context(nc.sbuf_tensor("kfm", [128, 8 * 512], BF16))
        kvblk = ctx.enter_context(nc.sbuf_tensor("kvblk", [128, 8 * 128], BF16))
        ksel = ctx.enter_context(nc.sbuf_tensor("ksel", [128, 145], BF16))
        scorebuf = ctx.enter_context(nc.sbuf_tensor("scorebuf", [128, NCH * 16], F32))
        combtm = ctx.enter_context(nc.sbuf_tensor("combtm", [128, NCH * 16], BF16))
        gbtile = ctx.enter_context(nc.sbuf_tensor("gbtile", [128, E], F32))
        gwtb = ctx.enter_context(nc.sbuf_tensor("gwtb", [128, 8 * 16], BF16))
        aux16 = ctx.enter_context(nc.sbuf_tensor("aux16", [16, TC], BF16))
        gscb = aux16
        rden = aux16
        combfm = aux16
        projblk = ctx.enter_context(nc.sbuf_tensor("projblk", [128, 128], BF16))
        ident = ctx.enter_context(nc.sbuf_tensor("ident", [128, 128], BF16))
        oneh = ctx.enter_context(nc.sbuf_tensor("oneh", [16, 2048], BF16))
        onecol = ctx.enter_context(nc.sbuf_tensor("onecol", [128, 1], BF16))
        borow = kftm
        stA = ctx.enter_context(nc.sbuf_tensor("stA", [128, 16], F32))
        stB = ctx.enter_context(nc.sbuf_tensor("stB", [128, 16], F32))
        stC = ctx.enter_context(nc.sbuf_tensor("stC", [128, 16], F32))
        stM = ctx.enter_context(nc.sbuf_tensor("stM", [128, 16], F32))
        stR = ctx.enter_context(nc.sbuf_tensor("stR", [128, 16], F32))
        stT = ctx.enter_context(nc.sbuf_tensor("stT", [128, 16], F32))
        stU = ctx.enter_context(nc.sbuf_tensor("stU", [128, 16], F32))
        stV2 = ctx.enter_context(nc.sbuf_tensor("stV2", [128, 16], F32))
        rtA = ctx.enter_context(nc.sbuf_tensor("rtA", [128, 88], F32))
        rtB = ctx.enter_context(nc.sbuf_tensor("rtB", [128, 88], F32))
        rt_g, rt_g2, rt_g8, rt_m8 = rtA[:, 0:4], rtA[:, 4:8], rtA[:, 8:16], rtA[:, 16:24]
        rt_keep, rt_t, rt_sm, rt_sel = rtA[:, 24:40], rtA[:, 40:56], rtA[:, 56:72], rtA[:, 72:88]
        rt2_g, rt2_g2, rt2_g8, rt2_m8 = rtB[:, 0:4], rtB[:, 4:8], rtB[:, 8:16], rtB[:, 16:24]
        rt2_keep, rt2_t, rt2_sm, rt2_sel = rtB[:, 24:40], rtB[:, 40:56], rtB[:, 56:72], rtB[:, 72:88]
        p0 = ctx.enter_context(nc.psum_tensor("p0", [128, 512], F32))
        p1 = ctx.enter_context(nc.psum_tensor("p1", [128, 512], F32))
        p2 = ctx.enter_context(nc.psum_tensor("p2", [128, 512], F32))
        p3 = ctx.enter_context(nc.psum_tensor("p3", [128, 512], F32))
        p4 = ctx.enter_context(nc.psum_tensor("p4", [128, 512], F32))
        p5 = ctx.enter_context(nc.psum_tensor("p5", [128, 512], F32))
        p6 = ctx.enter_context(nc.psum_tensor("p6", [128, 512], F32))
        pT = ctx.enter_context(nc.psum_tensor("pT", [128, 512], F32))
        sDG = ctx.enter_context(nc.semaphore("DG"))
        sDS = ctx.enter_context(nc.semaphore("DS"))
        sDO = ctx.enter_context(nc.semaphore("DO"))
        sT = ctx.enter_context(nc.semaphore("T"))
        sV = ctx.enter_context(nc.semaphore("V"))
        sC = ctx.enter_context(nc.semaphore("C"))
        sG = ctx.enter_context(nc.semaphore("G"))
        block = ctx.enter_context(nc.Block())
        S = Sched(nc)
        for nm, h in (("DG", sDG), ("DS", sDS), ("DO", sDO), ("T", sT),
                      ("V", sV), ("C", sC), ("G", sG)):
            S.sem(nm, h)
        P = [p0, p1, p2, p3, p4, p5, p6]
        lastV = {b: 0 for b in range(7)}
        lastC = {b: 0 for b in range(7)}

        def wait_bank(b):
            if lastV[b]:
                S.wait("tensor", "V", lastV[b])
            if lastC[b]:
                S.wait("tensor", "C", lastC[b])

        S.emit("sync", lambda e: e.dma_start(out=gbtile[:], in_=gbias), inc="DS", k=16)
        S.emit("sync", lambda e: e.dma_start(out=ident[:], in_=identc), inc="DS", k=16)
        S.emit("sync", lambda e: e.dma_start(out=oneh[:], in_=onehc), inc="DS", k=16)
        for q in range(16):
            S.emit("sync", lambda e, q=q: e.dma_start(
                out=resid[:, q * 1024:(q + 1) * 1024],
                in_=xo[q * 128:(q + 1) * 128, :]), inc="DS", k=16)

        S.emit("gpsimd", lambda e: e.dma_start(out=projblk[0:64, 0:64], in_=pj), inc="DG", k=16)
        S.emit("gpsimd", lambda e: e.dma_start(out=projblk[64:128, 64:128], in_=pj), inc="DG", k=16)
        S.emit("gpsimd", lambda e: e.dma_start(
            out=gwtb[:].rearrange("p (c j) -> p c j", j=E),
            in_=gwt.rearrange("(c p) j -> p c j", p=128)), inc="DG", k=16)
        dg_small = S.count("DG")
        S.emit("gpsimd", lambda e: e.dma_start(
            out=wA[:, 0:8192].rearrange("p (c j) -> p c j", j=D),
            in_=wk.rearrange("(c p) j -> p c j", p=128)), inc="DG", k=16)
        dg_wk = S.count("DG")
        S.emit("gpsimd", lambda e: e.dma_start(
            out=wB[:, 0:8192].rearrange("p (c j) -> p c j", j=D),
            in_=wv.rearrange("(c p) j -> p c j", p=128)), inc="DG", k=16)
        dg_wv = S.count("DG")
        S.emit("gpsimd", lambda e: e.dma_start(
            out=wA[:, 8192:12288].rearrange("p (c j) -> p c j", j=512),
            in_=wq[:, 0:512].rearrange("(c p) j -> p c j", p=128)), inc="DG", k=16)
        S.emit("gpsimd", lambda e: e.dma_start(
            out=wB[:, 8192:12288].rearrange("p (c j) -> p c j", j=512),
            in_=wq[:, 512:1024].rearrange("(c p) j -> p c j", p=128)), inc="DG", k=16)
        dg_w0 = S.count("DG")

        S.wait("vector", "DS", 48)
        S.emit("vector", lambda e: e.memset(onecol[:], 1.0))
        S.wait("vector", "DG", dg_small)
        S.emit("vector", lambda e: e.memset(projblk[0:64, 64:128], 0.0))
        S.emit("vector", lambda e: e.memset(projblk[64:128, 0:64], 0.0))
        S.emit("vector", lambda e: e.memset(kvblk[:], 0.0))
        S.emit("vector", lambda e: e.memset(ksel[:], 0.0))
        S.emit("vector", lambda e: e.memset(rtA[:, 12:16], -BIG))
        S.emit("vector", lambda e: e.memset(rtB[:, 12:16], -BIG))
        identf = scorebuf[:, 0:128]
        S.emit("vector", lambda e: e.tensor_copy(out=identf, in_=ident[:]))
        for c in range(8):
            S.emit("vector", lambda e, c=c: e.memset(
                vext[:, c * 129 + 128:c * 129 + 129], 1.0))
        for cs in range(7):
            S.emit("vector", lambda e, cs=cs: e.memset(
                xcast[:, 1024 + cs * 129 + 128:1024 + cs * 129 + 129], 1.0))
        S.emit("vector", lambda e: e.memset(ksel[:, 128:129], 1.0), inc="V")
        v_consts = S.count("V")

        t_xtr = [0] * NCH
        v_xfm_chunk = [0] * NCH
        v_pT = 0
        for i in range(NCH):
            if i == 0:
                S.wait("tensor", "V", v_consts)
                for bk in (pT, p6):
                    for c in range(4):
                        S.emit("tensor", lambda e, bk=bk, c=c: e.matmul(
                            bk[:, c * 128:(c + 1) * 128], identf, identf,
                            is_transpose=True, start=(c == 0), stop=(c == 3)))
            S.wait("tensor", "DS", 48 + 16 * (i + 1))
            for half in range(2):
                if v_pT:
                    S.wait("tensor", "V", v_pT)
                for c4 in range(4):
                    c = half * 4 + c4
                    S.emit("tensor", lambda e, i=i, c=c, c4=c4: e.matmul(
                        p6[:, c4 * 128:(c4 + 1) * 128],
                        resid[:, i * 1024 + c * 128:i * 1024 + (c + 1) * 128],
                        identf, is_transpose=True, start=(c4 == 0), stop=(c4 == 3)),
                        inc=("T" if c4 == 3 else None))
                tb = S.count("T")
                S.wait("vector", "T", tb)
                S.emit("vector", lambda e, i=i, half=half: e.tensor_copy(
                    out=xfm[:].rearrange("p (c t) -> p c t", t=TC)
                        [:, half * 4:(half + 1) * 4, i * 128:(i + 1) * 128],
                    in_=p6[:].rearrange("p (c t) -> p c t", t=128)), inc="V")
                v_pT = S.count("V")
            t_xtr[i] = S.count("T")
            v_xfm_chunk[i] = S.count("V")
        v_xfm_done = S.count("V")
        if DEBUG:
            S.wait("sync", "V", v_xfm_done)
            if DEBUG_XFM:
                S.emit("sync", lambda e: e.dma_start(out=dbg["xfm"], in_=xfm[:]), inc="DO", k=16)

        kv_bank = {0: 4, 1: 4, 2: 4, 3: 5, 4: 5, 5: 5, 6: 6, 7: 6}
        kv_off = {0: 0, 1: 170, 2: 340, 3: 0, 4: 170, 5: 340, 6: 0, 7: 170}
        kv_first = {4: 0, 5: 3, 6: 6}
        kv_last = {4: 2, 5: 5, 6: 7}

        t_kv_last = 0
        t_kf_last = 0
        kv_pend = [None]
        odd_ones = [False]
        t_kv_by_par = {0: 0, 1: 0}
        v_kvevac = [0]

        def kf_buf(par):
            return kftm if par == 0 else xcast[:, 0:1024]

        def emit_kv(pend):
            nonlocal t_kv_last
            ptile, ptc, v_ready = pend
            gtc = ptile * 4 + ptc
            par = ptc % 2
            S.wait("tensor", "V", v_ready)
            if v_kvevac[0]:
                S.wait("tensor", "V", v_kvevac[0])
            kb = kf_buf(par)
            vsrc = (lambda c: vext[:, c * 129:c * 129 + 129]) if par == 0 else \
                   (lambda c: xcast[:, 1024 + c * 129:1024 + c * 129 + 129]
                    if c < 7 else ksel[:, 0:129])
            for c in range(8):
                first = (kv_first[kv_bank[c]] == c)
                last = (kv_last[kv_bank[c]] == c)
                S.emit("tensor", lambda e, c=c, kb=kb, vs=vsrc(c), first=first,
                       last=last: e.matmul(
                    P[kv_bank[c]][:, kv_off[c]:kv_off[c] + KVW],
                    kb[:, c * 128:(c + 1) * 128], vs[:, 0:KVW],
                    start=first, stop=last, skip_group_check=True),
                    inc=("T" if c == 7 else None))
            t_kv_last = S.count("T")
            t_kv_by_par[par] = t_kv_last
            if gtc < 15:
                S.wait("vector", "T", t_kv_last)
                for b, nj in ((4, 3), (5, 3), (6, 2)):
                    base = {4: 0, 5: 3, 6: 6}[b]
                    S.emit("vector", lambda e, gtc=gtc, b=b, nj=nj, base=base: e.tensor_copy(
                        out=big0[:, gtc * 1032 + base * 129: gtc * 1032 + (base + nj) * 129]
                            .rearrange("p (j k) -> p j k", k=129),
                        in_=P[b][:, 0:nj * 170]
                            .rearrange("p (j k) -> p j k", k=170)[:, :, 0:129]),
                        inc=("V" if b == 6 else None))
                v_kvevac[0] = S.count("V")
                for b in (4, 5, 6):
                    lastV[b] = v_kvevac[0]
                if DEBUG and AMPUTATE and SNAPDUMP and gtc == 0:
                    S.wait("sync", "V", v_kvevac[0])
                    S.emit("sync", lambda e: e.dma_start(
                        out=dbg["kftm"], in_=kftm[:]), inc="DO", k=16)
                    S.emit("sync", lambda e: e.dma_start(
                        out=dbg["vext"][:, 0:1032], in_=vext[:]), inc="DO", k=16)
                    S.emit("sync", lambda e: e.dma_start(
                        out=dbg["outfm"][:, 0:1024], in_=kftm[:]), inc="DO", k=16)
                    S.emit("sync", lambda e: e.dma_start(
                        out=dbg["outfm"][:, 1024:2056], in_=vext[:]), inc="DO", k=16)
                    S.wait("vector", "DO", S.count("DO"))
        S.wait("tensor", "DG", dg_wk)
        for bknum in (4, 5, 6):
            S.emit("tensor", lambda e, bknum=bknum: e.matmul(
                P[bknum][:, 0:512], wA[:, 0:128], wA[:, 0:512],
                start=True, stop=True))
        if WARMUP:
            for r in range(WARMUP if WARMUP > 1 else 15):
                for dc in range(8):
                    S.emit("tensor", lambda e, dc=dc: e.matmul(
                        P[0][:], wA[:, dc * 1024:dc * 1024 + 128], wA[:, 0:512],
                        start=(dc == 0), stop=(dc == 7)))

        for tile in range(4):
            own = True
            buf = tile % 2
            S.wait("tensor", "V", v_xfm_chunk[4 * tile + 3])
            if kv_pend[0] is not None:
                emit_kv(kv_pend[0])
                kv_pend[0] = None

            def rhs512(dc, tile=tile):
                return xfm[:, dc * TC + tile * 512: dc * TC + tile * 512 + 512]

            def blk128(dc, tc, tile=tile):
                o = dc * TC + tile * 512 + tc * 128
                return xfm[:, o:o + 128]

            for c in range(8):
                bank = c % 2
                wait_bank(bank)
                for dc in range(8):
                    rhs = rhs512(dc)
                    S.emit("tensor", lambda e, bank=bank, dc=dc, c=c, rhs=rhs: e.matmul(
                        P[bank][:], wA[:, dc * 1024 + c * 128: dc * 1024 + (c + 1) * 128],
                        rhs, start=(dc == 0), stop=(dc == 7)),
                        inc=("T" if dc == 7 else None))
                t_k = S.count("T")
                S.wait("vector", "T", t_k)
                if t_kf_last:
                    S.wait("vector", "T", t_kf_last)
                S.emit("vector", lambda e, bank=bank, c=c: e.tensor_copy(
                    out=kfm[:, c * 512:(c + 1) * 512], in_=P[bank][:]), inc="V")
                lastV[bank] = S.count("V")

            for tc in range(4):
                wait_bank(0)
                wait_bank(1)
                if kv_pend[0] is not None:
                    pass
                for c in range(8):
                    bank = 0 if c < 4 else 1
                    S.emit("tensor", lambda e, c=c, bank=bank, tc=tc: e.matmul(
                        P[bank][:, (c % 4) * 128:((c % 4) + 1) * 128],
                        kfm[:, c * 512 + tc * 128: c * 512 + tc * 128 + 128],
                        projblk[:], start=(c % 4 == 0), stop=(c % 4 == 3)),
                        inc=("T" if c % 4 == 3 else None))
                t_kf_last = S.count("T")

                if tile == 0 and tc == 0:
                    S.wait("tensor", "DG", dg_wv)
                wait_bank(2)
                wait_bank(3)
                for dc in range(8):
                    blk = blk128(dc, tc)
                    for half, bank in ((0, 2), (1, 3)):
                        S.emit("tensor", lambda e, blk=blk, dc=dc, half=half, bank=bank: e.matmul(
                            P[bank][:], blk,
                            wB[:, dc * 1024 + half * 512: dc * 1024 + (half + 1) * 512],
                            start=(dc == 0), stop=(dc == 7)),
                            inc=("T" if (dc == 7 and half == 1) else None))
                t_v = S.count("T")

                if kv_pend[0] is not None:
                    emit_kv(kv_pend[0])


                par = tc % 2
                kb = kf_buf(par)
                S.wait("vector", "T", t_kf_last)
                if t_kv_by_par[par]:
                    S.wait("vector", "T", t_kv_by_par[par])
                for bank in (0, 1):
                    S.emit("vector", lambda e, kb=kb, bank=bank: e.tensor_scalar(
                        out=kb[:, bank * 512:(bank + 1) * 512], in0=P[bank][:],
                        scalar1=EPS_A, scalar2=EPS_A, op0=OP.add, op1=OP.max), inc="V")
                    lastV[bank] = S.count("V")
                S.wait("vector", "T", t_v)
                if par == 0:
                    for half, bank in ((0, 2), (1, 3)):
                        S.emit("vector", lambda e, half=half, bank=bank: e.tensor_copy(
                            out=vext[:].rearrange("p (s j) -> p s j", j=129)
                                [:, half * 4:(half + 1) * 4, 0:128],
                            in_=P[bank][:].rearrange("p (s j) -> p s j", j=128)), inc="V")
                        lastV[bank] = S.count("V")
                else:
                    S.emit("vector", lambda e: e.tensor_copy(
                        out=xcast[:, 1024:1024 + 4 * 129]
                            .rearrange("p (s j) -> p s j", j=129)[:, :, 0:128],
                        in_=P[2][:].rearrange("p (s j) -> p s j", j=128)), inc="V")
                    lastV[2] = S.count("V")
                    S.emit("vector", lambda e: e.tensor_copy(
                        out=xcast[:, 1024 + 4 * 129:1024 + 7 * 129]
                            .rearrange("p (s j) -> p s j", j=129)[:, :, 0:128],
                        in_=P[3][:, 0:384].rearrange("p (s j) -> p s j", j=128)), inc="V")
                    S.emit("vector", lambda e: e.tensor_copy(
                        out=ksel[:, 0:128], in_=P[3][:, 384:512]), inc="V")
                    lastV[3] = S.count("V")
                v_now = S.count("V")
                kv_pend[0] = (tile, tc, v_now)

        if kv_pend[0] is not None:
            emit_kv(kv_pend[0])
            kv_pend[0] = None
        t_ph1 = S.count("T")

        S.wait("vector", "T", t_ph1)
        S.emit("vector", lambda e: e.memset(ksel[:, 0:128], 0.0))
        if DEBUG:
            for b in (4, 5, 6):
                S.emit("vector", lambda e, b=b: e.tensor_copy(
                    out=kfm[:, (b - 4) * 512:(b - 4) * 512 + 510],
                    in_=P[b][:, 0:510]), inc="V")
            S.wait("sync", "V", S.count("V"))
            S.emit("sync", lambda e: e.dma_start(
                out=dbg["kvparts"], in_=big0[:, 0:15 * 1032]), inc="DO", k=16)
            S.emit("sync", lambda e: e.dma_start(
                out=dbg["kvp456"][:, 0:1536], in_=kfm[:, 0:1536]), inc="DO", k=16)
            S.wait("vector", "DO", S.count("DO"))
        if AMPUTATE:
            S.wait("sync", "DO")
            with nc.allow_low_precision(reason="debug amputated build"):
                S.run(block)
            return nc
        for lv_step in ((1, 2), (2, 4), (4, 8), (8, 16)):
            d, stride = lv_step
            for a0 in range(0, 15, stride):
                b0 = a0 + d
                if b0 > 14:
                    continue
                S.emit("vector", lambda e, a0=a0, b0=b0: e.tensor_tensor(
                    out=big0[:, a0 * 1032:(a0 + 1) * 1032],
                    in0=big0[:, a0 * 1032:(a0 + 1) * 1032],
                    in1=big0[:, b0 * 1032:(b0 + 1) * 1032], op=OP.add))
        S.drain("vector")
        for b, nj in ((4, 3), (5, 3), (6, 2)):
            base = {4: 0, 5: 3, 6: 6}[b]
            S.emit("vector", lambda e, b=b, nj=nj, base=base: e.tensor_tensor(
                out=big0[:, base * 129:(base + nj) * 129].rearrange("p (j k) -> p j k", k=129),
                in0=P[b][:, 0:nj * 170].rearrange("p (j k) -> p j k", k=170)[:, :, 0:129],
                in1=big0[:, base * 129:(base + nj) * 129].rearrange("p (j k) -> p j k", k=129),
                op=OP.add), inc=("V" if b == 6 else None))
        v_kvstage = S.count("V")
        for b in (4, 5, 6):
            lastV[b] = v_kvstage

        S.wait("gpsimd", "V", v_kvstage)
        S.emit("gpsimd", lambda e: e.dma_start(out=kv_ci, in_=big0[:, 0:1032]), inc="DG", k=16)
        dg_ci = S.count("DG")
        S.wait("gpsimd", "DG", dg_ci)
        S.emit("gpsimd", lambda e: e.collective_compute(
            "AllReduce", mybir.AluOpType.add, ins=[kv_ci], outs=[kv_co],
            replica_groups=[[0, 1], [2, 3], [4, 5], [6, 7]]), inc="G")
        g_cc = S.count("G")
        S.wait("gpsimd", "G", g_cc)
        S.emit("gpsimd", lambda e: e.dma_start(
            out=kvblk[0:64, 0:1024].rearrange("p (j k) -> p j k", k=128)[:, :, 0:64],
            in_=kv_co[0:64, 0:1032].rearrange("p (j k) -> p j k", k=129)[:, :, 0:64]),
            inc="DG", k=16)
        S.emit("gpsimd", lambda e: e.dma_start(
            out=kvblk[64:128, 0:1024].rearrange("p (j k) -> p j k", k=128)[:, :, 64:128],
            in_=kv_co[64:128, 0:1032].rearrange("p (j k) -> p j k", k=129)[:, :, 64:128]),
            inc="DG", k=16)
        S.emit("gpsimd", lambda e: e.dma_start(
            out=ksel[0:64, 0:144].rearrange("p (j k) -> p j k", k=18)[:, :, 0:1],
            in_=kv_co[0:64, 0:1032].rearrange("p (j k) -> p j k", k=129)[:, :, 128:129],
            allow_slow_non_contiguous=True), inc="DG", k=16)
        S.emit("gpsimd", lambda e: e.dma_start(
            out=ksel[64:128, 1:145].rearrange("p (j k) -> p j k", k=18)[:, :, 0:1],
            in_=kv_co[64:128, 0:1032].rearrange("p (j k) -> p j k", k=129)[:, :, 128:129],
            allow_slow_non_contiguous=True), inc="DG", k=16)
        dg_back = S.count("DG")
        if DEBUG:
            S.wait("sync", "V", v_kvstage)
            S.emit("sync", lambda e: e.dma_start(out=dbg["kvstage"], in_=big0[:, 0:1032]), inc="DO", k=16)
            S.wait("sync", "DG", dg_back)
            S.emit("sync", lambda e: e.dma_start(out=dbg["kvblk"], in_=kvblk[:]), inc="DO", k=16)
            S.emit("sync", lambda e: e.dma_start(out=dbg["ksel"], in_=ksel[:]), inc="DO", k=16)
            S.wait("vector", "DO")

        S.wait("gpsimd", "T", t_ph1)
        S.emit("gpsimd", lambda e: e.dma_start(out=borow[0:1, 0:D], in_=bo), inc="DG", k=16)
        S.emit("gpsimd", lambda e: e.dma_start(
            out=wA[:, 0:8192].rearrange("p (c j) -> p c j", j=D),
            in_=wo.rearrange("(c p) j -> p c j", p=128)), inc="DG", k=16)
        dg_wo = S.count("DG")

        S.wait("tensor", "DG", dg_w0)
        c_qevac = [0, 0, 0, 0]
        t_qfmm = [0, 0, 0, 0]
        t_q_last = 0
        for c in range(8):
            wqbuf = wA if c < 4 else wB
            for tl in range(4):
                wait_bank(tl)
                if c_qevac[tl]:
                    S.wait("tensor", "C", c_qevac[tl])
                for dc in range(8):
                    S.emit("tensor", lambda e, c=c, tl=tl, dc=dc, wqbuf=wqbuf: e.matmul(
                        P[tl][:], wqbuf[:, 8192 + dc * 512 + (c % 4) * 128: 8192 + dc * 512 + (c % 4 + 1) * 128],
                        xfm[:, dc * TC + tl * 512: dc * TC + (tl + 1) * 512],
                        start=(dc == 0), stop=(dc == 7)),
                        inc=("T" if dc == 7 else None))
            t_q = S.count("T")
            t_q_last = t_q
            S.wait("scalar", "T", t_q)
            for tl in range(4):
                if t_qfmm[tl]:
                    S.wait("scalar", "T", t_qfmm[tl])
                S.emit("scalar", lambda e, tl=tl: e.copy(
                    out=xcast[:, tl * 512:(tl + 1) * 512], in_=P[tl][:]), inc="C")
                c_qevac[tl] = S.count("C")
                lastC[tl] = c_qevac[tl]
            for tl in range(4):
                bank = 4 + (tl % 2)
                wait_bank(bank)
                S.wait("tensor", "C", c_qevac[tl])
                S.emit("tensor", lambda e, tl=tl, bank=bank: e.matmul(
                    P[bank][:], projblk[:], xcast[:, tl * 512:(tl + 1) * 512],
                    start=True, stop=True), inc="T")
                t_qfmm[tl] = S.count("T")
                S.wait("vector", "T", t_qfmm[tl])
                if c == 0 and tl == 0:
                    S.wait("vector", "DG", dg_ci)
                S.emit("vector", lambda e, c=c, tl=tl, bank=bank: e.tensor_scalar(
                    out=big0[:, c * TC + tl * 512: c * TC + (tl + 1) * 512],
                    in0=P[bank][:], scalar1=EPS_A, scalar2=EPS_A,
                    op0=OP.add, op1=OP.max), inc="V")
                lastV[bank] = S.count("V")
        v_qf_done = S.count("V")

        S.wait("tensor", "V", v_qf_done)
        S.wait("tensor", "DG", dg_back)
        for tl in range(4):
            wait_bank(6)
            for pr in range(8):
                S.emit("tensor", lambda e, pr=pr, tl=tl: e.matmul(
                    p6[0:16, :], ksel[:, pr * 16:(pr + 1) * 16],
                    big0[:, pr * TC + tl * 512: pr * TC + (tl + 1) * 512],
                    start=(pr == 0), stop=(pr == 7)),
                    inc=("T" if pr == 7 else None))
            t_den = S.count("T")
            S.wait("vector", "T", t_den)
            S.emit("vector", lambda e: e.tensor_scalar(
                out=p5[0:16, :], in0=p6[0:16, :], scalar1=EPS_A, scalar2=None, op0=OP.add))
            S.drain("vector")
            S.emit("vector", lambda e, tl=tl: e.reciprocal(
                out=rden[:, tl * 512:(tl + 1) * 512], in_=p5[0:16, :]), inc="V")
            lastV[6] = S.count("V")

        v_rden = S.count("V")
        S.wait("tensor", "V", v_rden)
        for c in range(8):
            for tl in range(4):
                bn = tl % 2
                bb = 2 + (tl % 2)
                wait_bank(bn)
                wait_bank(bb)
                S.emit("tensor", lambda e, c=c, tl=tl, bn=bn: e.matmul(
                    P[bn][:], kvblk[:, c * 128:(c + 1) * 128],
                    big0[:, c * TC + tl * 512: c * TC + (tl + 1) * 512],
                    start=True, stop=True))
                S.emit("tensor", lambda e, c=c, tl=tl, bb=bb: e.matmul(
                    P[bb][0:64, :], oneh[:, (2 * c) * 128:(2 * c) * 128 + 64],
                    rden[:, tl * 512:(tl + 1) * 512],
                    start=True, stop=True))
                S.emit("tensor", lambda e, c=c, tl=tl, bb=bb: e.matmul(
                    P[bb][64:128, :], oneh[:, (2 * c + 1) * 128 + 64:(2 * c + 1) * 128 + 128],
                    rden[:, tl * 512:(tl + 1) * 512],
                    start=True, stop=True), inc="T")
                t_nb = S.count("T")
                S.wait("vector", "T", t_nb)
                S.emit("vector", lambda e, tl=tl, bb=bb: e.tensor_copy(
                    out=kfm[:, (tl % 2) * 512:(tl % 2 + 1) * 512], in_=P[bb][:]))
                S.emit("vector", lambda e, c=c, tl=tl, bn=bn: e.tensor_tensor(
                    out=big0[:, c * TC + tl * 512: c * TC + (tl + 1) * 512],
                    in0=P[bn][:], in1=kfm[:, (tl % 2) * 512:(tl % 2 + 1) * 512],
                    op=OP.mult), inc="V")
                lastV[bn] = S.count("V")
                lastV[bb] = S.count("V")
        v_outfm = S.count("V")
        if DEBUG:
            S.wait("sync", "V", v_outfm)
            S.emit("sync", lambda e: e.dma_start(out=dbg["outfm"], in_=big0[:]), inc="DO", k=16)
            S.wait("vector", "DO")

        S.wait("tensor", "DG", dg_wo)
        S.wait("tensor", "V", v_outfm)
        S.wait("vector", "DS", 64)
        S.wait("scalar", "DS", 64)
        t_attn_last = 0
        for i in range(NCH):
            b0, b1 = (0, 1) if i % 2 == 0 else (2, 3)
            wait_bank(b0)
            wait_bank(b1)
            for ic in range(8):
                for half, b in ((0, b0), (1, b1)):
                    S.emit("tensor", lambda e, i=i, ic=ic, half=half, b=b: e.matmul(
                        P[b][:], big0[:, ic * TC + i * 128: ic * TC + (i + 1) * 128],
                        wA[:, ic * 1024 + half * 512: ic * 1024 + (half + 1) * 512],
                        start=(ic == 0), stop=False))
            for half, b in ((0, b0), (1, b1)):
                S.emit("tensor", lambda e, half=half, b=b: e.matmul(
                    P[b][:], oneh[0:1, 0:128], borow[0:1, half * 512:(half + 1) * 512],
                    start=False, stop=True), inc=("T" if half == 1 else None))
            t_attn_last = S.count("T")
            S.wait("vector", "T", t_attn_last)
            for half, b in ((0, b0), (1, b1)):
                st_dst = stA if half == 0 else stT
                S.emit("vector", lambda e, i=i, half=half, b=b, st_dst=st_dst: e.scalar_tensor_tensor(
                    out=resid[:, i * 1024 + half * 512: i * 1024 + half * 512 + 512],
                    in0=P[b][:], scalar=1.0,
                    in1=resid[:, i * 1024 + half * 512: i * 1024 + half * 512 + 512],
                    op0=OP.mult, op1=OP.add,
                    accum_out=st_dst[:, i:i + 1]), inc="V")
            lastV[b0] = S.count("V")
            lastV[b1] = S.count("V")
            v_res = S.count("V")
            S.wait("scalar", "V", v_res)
            S.emit("scalar", lambda e, i=i: e.activation(
                out=p4[:], in_=resid[:, i * 1024: i * 1024 + 512], func=AF.Square,
                accum_out=stB[:, i:i + 1]))
            S.emit("scalar", lambda e, i=i: e.activation(
                out=p5[:], in_=resid[:, i * 1024 + 512:(i + 1) * 1024], func=AF.Square,
                accum_out=stC[:, i:i + 1]), inc="C")
        c_sq1 = S.count("C")
        lastC[4] = c_sq1
        lastC[5] = c_sq1

        def ln_smalls():
            S.emit("vector", lambda e: e.tensor_tensor(out=stU[:], in0=stA[:], in1=stT[:], op=OP.add))
            S.drain("vector")
            S.emit("vector", lambda e: e.tensor_scalar(out=stM[:], in0=stU[:], scalar1=1.0 / D,
                                                       scalar2=None, op0=OP.mult))
            S.emit("vector", lambda e: e.tensor_tensor(out=stU[:], in0=stB[:], in1=stC[:], op=OP.add))
            S.drain("vector")
            S.emit("vector", lambda e: e.tensor_scalar(out=stU[:], in0=stU[:], scalar1=1.0 / D,
                                                       scalar2=None, op0=OP.mult))
            S.drain("vector")
            S.emit("vector", lambda e: e.tensor_tensor(out=stV2[:], in0=stM[:], in1=stM[:], op=OP.mult))
            S.drain("vector")
            S.emit("vector", lambda e: e.tensor_tensor(out=stU[:], in0=stU[:], in1=stV2[:], op=OP.subtract))
            S.drain("vector")
            S.emit("vector", lambda e: e.tensor_scalar(out=stU[:], in0=stU[:], scalar1=EPS_LN,
                                                       scalar2=None, op0=OP.add), inc="V")
            v_var = S.count("V")
            S.wait("scalar", "V", v_var)
            S.emit("scalar", lambda e: e.activation(out=stV2[:], in_=stU[:], func=AF.Sqrt), inc="C")
            c_sqrt = S.count("C")
            S.wait("vector", "C", c_sqrt)
            S.emit("vector", lambda e: e.reciprocal(out=stR[:], in_=stV2[:]))
            S.drain("vector")
            S.emit("vector", lambda e: e.tensor_tensor(out=stT[:], in0=stR[:], in1=stR[:], op=OP.mult))
            S.drain("vector")
            S.emit("vector", lambda e: e.tensor_tensor(out=stT[:], in0=stT[:], in1=stU[:], op=OP.mult))
            S.drain("vector")
            S.emit("vector", lambda e: e.tensor_scalar(out=stT[:], in0=stT[:], scalar1=-0.5,
                                                       scalar2=1.5, op0=OP.mult, op1=OP.add))
            S.drain("vector")
            S.emit("vector", lambda e: e.tensor_tensor(out=stR[:], in0=stR[:], in1=stT[:], op=OP.mult),
                   inc="V")
            S.drain("vector")

        S.wait("vector", "C", c_sq1)
        ln_smalls()

        t_x1tr = [0] * NCH
        v_x1_chunk = [0] * NCH
        c_apply = [0] * NCH
        S.emit("vector", lambda e: e.scalar_tensor_tensor(
            out=stT[:], in0=stM[:], scalar=-1.0, op0=OP.mult,
            in1=stR[:], op1=OP.mult), inc="V")
        v_lnp = S.count("V")
        S.wait("scalar", "V", v_lnp)
        for i in range(NCH):
            S.emit("scalar", lambda e, i=i: e.activation(
                out=resid[:, i * 1024:(i + 1) * 1024],
                in_=resid[:, i * 1024:(i + 1) * 1024],
                func=AF.Identity, bias=stT[:, i:i + 1],
                scale=stR[:, i:i + 1]), inc="C")
            c_apply[i] = S.count("C")
        for i in range(NCH):
            S.wait("tensor", "C", c_apply[i])
            for half in range(2):
                if v_pT:
                    S.wait("tensor", "V", v_pT)
                for c4 in range(4):
                    c = half * 4 + c4
                    S.emit("tensor", lambda e, i=i, c=c, c4=c4: e.matmul(
                        pT[:, c4 * 128:(c4 + 1) * 128],
                        resid[:, i * 1024 + c * 128:i * 1024 + (c + 1) * 128],
                        identf, is_transpose=True, start=(c4 == 0), stop=(c4 == 3)),
                        inc=("T" if c4 == 3 else None))
                tb = S.count("T")
                S.wait("vector", "T", tb)
                S.emit("vector", lambda e, i=i, half=half: e.tensor_copy(
                    out=xfm[:].rearrange("p (c t) -> p c t", t=TC)
                        [:, half * 4:(half + 1) * 4, i * 128:(i + 1) * 128],
                    in_=pT[:].rearrange("p (c t) -> p c t", t=128)), inc="V")
                v_pT = S.count("V")
            t_x1tr[i] = S.count("T")
            v_x1_chunk[i] = S.count("V")
        v_x1fm = S.count("V")
        if DEBUG:
            S.wait("sync", "V", v_x1fm)
            S.emit("sync", lambda e: e.dma_start(
                out=dbg["x1"].rearrange("(i p) d -> p i d", p=128),
                in_=resid[:].rearrange("p (i d) -> p i d", d=D)), inc="DO", k=16)
            S.wait("vector", "DO")

        for tl in range(4):
            S.wait("tensor", "V", v_x1_chunk[4 * tl + 3])
            wait_bank(6)
            for dc in range(8):
                S.emit("tensor", lambda e, tl=tl, dc=dc: e.matmul(
                    p6[0:16, :], gwtb[:, dc * 16:(dc + 1) * 16],
                    xfm[:, dc * TC + tl * 512: dc * TC + (tl + 1) * 512],
                    start=(dc == 0), stop=(dc == 7)), inc=("T" if dc == 7 else None))
            t_g = S.count("T")
            S.wait("vector", "T", t_g)
            S.emit("vector", lambda e, tl=tl: e.tensor_copy(
                out=gscb[:, tl * 512:(tl + 1) * 512], in_=p6[0:16, :]), inc="V")
            lastV[6] = S.count("V")
        v_logits = S.count("V")

        S.wait("tensor", "V", v_logits)
        c_scgrp = 0
        for g in range(2):
            if v_pT:
                S.wait("tensor", "V", v_pT)
            if c_scgrp:
                S.wait("tensor", "C", c_scgrp)
            for q in range(8):
                i = g * 8 + q
                S.emit("tensor", lambda e, i=i, q=q: e.matmul(
                    pT[:, q * 16:(q + 1) * 16],
                    gscb[:, i * 128:(i + 1) * 128],
                    ident[0:16, 0:16],
                    start=(q == 0), stop=(q == 7)), inc=("T" if q == 7 else None))
            t_sc = S.count("T")
            S.wait("scalar", "T", t_sc)
            S.emit("scalar", lambda e, g=g: e.activation(
                out=scorebuf[:, g * 128:(g + 1) * 128], in_=pT[:, 0:128],
                func=AF.Sigmoid), inc="C")
            c_scgrp = S.count("C")
        c_scores = S.count("C")
        if DEBUG:
            S.wait("sync", "C", c_scores)
            S.emit("sync", lambda e: e.dma_start(out=dbg["scores"], in_=scorebuf[:]), inc="DO", k=16)

        S.wait("vector", "C", c_scores)
        S.wait("vector", "DS", 64)
        LANES = [
            dict(t=rt_t, g=rt_g, g2=rt_g2, g8=rt_g8, m8=rt_m8, keep=rt_keep,
                 sm=rt_sm, sel=rt_sel),
            dict(t=rt2_t, g=rt2_g, g2=rt2_g2, g8=rt2_g8, m8=rt2_m8, keep=rt2_keep,
                 sm=rt2_sm, sel=rt2_sel),
        ]

        def r_steps(i, L):
            sc = scorebuf[:, i * 16:(i + 1) * 16]
            yield lambda e: e.tensor_tensor(out=L["t"][:], in0=sc, in1=gbtile[:], op=OP.add)
            yield lambda e: e.tensor_tensor(out=L["g"][:], in0=L["t"][:, 0:16:4],
                                            in1=L["t"][:, 1:16:4], op=OP.max)
            yield lambda e: e.tensor_tensor(out=L["g2"][:], in0=L["t"][:, 2:16:4],
                                            in1=L["t"][:, 3:16:4], op=OP.max)
            yield lambda e: e.tensor_tensor(out=L["g8"][:, 0:4], in0=L["g"][:],
                                            in1=L["g2"][:], op=OP.max)
            yield lambda e: e.max(out=L["m8"][:], in_=L["g8"][:])
            yield lambda e: e.tensor_scalar(out=L["g"][:], in0=L["g8"][:, 0:4],
                                            scalar1=L["m8"][:, 1:2], scalar2=None, op0=OP.is_ge)
            yield lambda e: e.tensor_copy(out=L["keep"][:, 0:16:4], in_=L["g"][:])
            yield lambda e: e.tensor_copy(out=L["keep"][:, 1:16:4], in_=L["g"][:])
            yield lambda e: e.tensor_copy(out=L["keep"][:, 2:16:4], in_=L["g"][:])
            yield lambda e: e.tensor_copy(out=L["keep"][:, 3:16:4], in_=L["g"][:])
            yield lambda e: e.tensor_scalar(out=L["sm"][:], in0=L["keep"][:], scalar1=BIG,
                                            scalar2=BIG, op0=OP.mult, op1=OP.subtract)
            yield lambda e: e.tensor_tensor(out=L["sm"][:], in0=L["t"][:], in1=L["sm"][:], op=OP.add)
            yield lambda e: e.max(out=L["m8"][:], in_=L["sm"][:])
            yield lambda e: e.tensor_scalar(out=L["sel"][:], in0=L["sm"][:],
                                            scalar1=L["m8"][:, 3:4], scalar2=None, op0=OP.is_ge)
            yield lambda e, i=i: e.tensor_tensor(out=combtm[:, i * 16:(i + 1) * 16],
                                                 in0=L["sel"][:], in1=sc, op=OP.mult)

        for pair in range(8):
            i0, i1 = 2 * pair, 2 * pair + 1
            gens = [r_steps(i0, LANES[0]), r_steps(i1, LANES[1])]
            ops0 = list(gens[0]); ops1 = list(gens[1])
            for k in range(len(ops0)):
                lastpair = (pair == 7 and k == len(ops0) - 1)
                S.emit("vector", ops0[k])
                S.emit("vector", ops1[k], inc=("V" if lastpair else None))
        v_comb = S.count("V")
        if DEBUG:
            S.wait("sync", "V", v_comb)
            S.emit("sync", lambda e: e.dma_start(out=dbg["comb"], in_=combtm[:]), inc="DO", k=16)

        S.wait("tensor", "V", v_comb)
        S.wait("tensor", "C", c_scores)
        for g in range(4):
            if v_pT:
                S.wait("tensor", "V", v_pT)
            for q in range(4):
                i = g * 4 + q
                S.emit("tensor", lambda e, i=i, q=q: e.matmul(
                    pT[0:16, q * 128:(q + 1) * 128],
                    combtm[:, i * 16:(i + 1) * 16],
                    ident[:],
                    start=(q == 0), stop=(q == 3)), inc=("T" if q == 3 else None))
            t_cf = S.count("T")
            S.wait("vector", "T", t_cf)
            S.emit("vector", lambda e, g=g: e.tensor_copy(
                out=combfm[:, g * 512:(g + 1) * 512], in_=pT[0:16, 0:512]), inc="V")
            v_pT = S.count("V")
        v_combfm = S.count("V")

        S.wait("vector", "C", c_apply[NCH - 1])
        SLOT = {
            "A0": wA[:, 0:4096], "A1": wA[:, 4096:8192], "A2": wA[:, 8192:12288],
            "B0": wB[:, 0:4096], "B1": wB[:, 4096:8192], "B2": wB[:, 8192:12288],
        }
        TRIPLE = {0: ("B0", "B1", "A2"), 1: ("B2", "A0", "A1")}
        slot_free = {
            "B0": t_ph1, "B1": t_ph1,
            "A2": t_q_last, "B2": t_q_last,
            "A0": t_attn_last, "A1": t_attn_last,
        }
        dg13 = [0] * 17
        dg2 = [0] * 17
        t_h3_done = [0] * 17

        def srcs_of(idx):
            if idx < 16:
                return w1t[idx], w3t[idx], w2t[idx]
            return ws1t, ws3t, ws2t

        def issue_dma13(idx):
            s1n, s3n, _ = TRIPLE[idx % 2]
            s1, s3, _ = srcs_of(idx)
            for sl, src in ((s1n, s1), (s3n, s3)):
                if slot_free[sl]:
                    S.wait("gpsimd", "T", slot_free[sl])
                S.emit("gpsimd", lambda e, dst=SLOT[sl], src=src: e.dma_start(
                    out=dst.rearrange("p (c j) -> p c j", j=INTER),
                    in_=src.rearrange("(c p) j -> p c j", p=128)), inc="DG", k=16)
            dg13[idx] = S.count("DG")

        def issue_dma2(idx):
            _, _, s2n = TRIPLE[idx % 2]
            _, _, s2 = srcs_of(idx)
            if slot_free[s2n]:
                S.wait("gpsimd", "T", slot_free[s2n])
            S.emit("gpsimd", lambda e, dst=SLOT[s2n], src=s2: e.dma_start(
                out=dst.rearrange("p (c j) -> p c j", j=D),
                in_=src.rearrange("(c p) j -> p c j", p=128)), inc="DG", k=16)
            dg2[idx] = S.count("DG")

        issue_dma13(0)
        issue_dma2(0)
        issue_dma13(1)
        issue_dma2(1)
        lastVkfm = [0]

        def ffn_expert(idx, slot):
            s1n, s3n, _ = TRIPLE[idx % 2]
            w1s, w3s = SLOT[s1n], SLOT[s3n]
            hbase = slot * 8192
            if slot == 0:
                cbp = [kfm[:, 2048 + t * 512:2048 + (t + 1) * 512] for t in range(4)]
            else:
                cbp = [kvblk[:, 0:512], kvblk[:, 512:1024], kftm[:, 0:512], kftm[:, 512:1024]]
            shared = (idx == 16)
            S.wait("tensor", "DG", dg13[idx])
            if not shared:
                for tl, b in ((0, 4), (1, 5), (2, 6)):
                    wait_bank(b)
                    S.emit("tensor", lambda e, idx=idx, tl=tl, b=b: e.matmul(
                        P[b][:], oneh[:, idx * 128:(idx + 1) * 128],
                        combfm[:, tl * 512:(tl + 1) * 512],
                        start=True, stop=True), inc=("T" if tl == 2 else None))
                t_cb1 = S.count("T")
                S.wait("vector", "T", t_cb1)
                v_cb0 = 0
                for tl, b in ((0, 4), (1, 5), (2, 6)):
                    S.emit("vector", lambda e, cb=cbp[tl], b=b: e.tensor_copy(
                        out=cb, in_=P[b][:]), inc="V")
                    lastV[b] = S.count("V")
                    if tl == 0:
                        v_cb0 = S.count("V")
                S.wait("tensor", "V", v_cb0)
                S.emit("tensor", lambda e, idx=idx: e.matmul(
                    P[4][:], oneh[:, idx * 128:(idx + 1) * 128],
                    combfm[:, 3 * 512:4 * 512],
                    start=True, stop=True), inc="T")
                t_cb2 = S.count("T")
                S.wait("vector", "T", t_cb2)
                S.emit("vector", lambda e, cb=cbp[3]: e.tensor_copy(
                    out=cb, in_=P[4][:]), inc="V")
                lastV[4] = S.count("V")
            TO = [1, 2, 3, 0]
            H3B = [4, 5, 6, 0]
            for ic in range(4):
                t_h1t = {}
                for tl in TO:
                    wait_bank(tl)
                    for dc in range(8):
                        S.emit("tensor", lambda e, w1s=w1s, ic=ic, tl=tl, dc=dc: e.matmul(
                            P[tl][:], w1s[:, dc * 512 + ic * 128: dc * 512 + (ic + 1) * 128],
                            xfm[:, dc * TC + tl * 512: dc * TC + (tl + 1) * 512],
                            start=(dc == 0), stop=(dc == 7)),
                            inc=("T" if dc == 7 else None))
                    t_h1t[tl] = S.count("T")
                if lastVkfm[0]:
                    S.wait("scalar", "V", lastVkfm[0])
                c_silut = {}
                for tl in TO:
                    S.wait("scalar", "T", t_h1t[tl])
                    S.emit("scalar", lambda e, tl=tl: e.activation(
                        out=kfm[:, tl * 512:(tl + 1) * 512], in_=P[tl][:], func=AF.Silu),
                        inc="C")
                    c_silut[tl] = S.count("C")
                    lastC[tl] = c_silut[tl]
                t_h3t = {}
                for k, tl in enumerate(TO):
                    b = H3B[k]
                    if b == 0:
                        S.wait("tensor", "C", c_silut[0])
                    wait_bank(b)
                    for dc in range(8):
                        S.emit("tensor", lambda e, w3s=w3s, ic=ic, tl=tl, dc=dc, b=b: e.matmul(
                            P[b][:], w3s[:, dc * 512 + ic * 128: dc * 512 + (ic + 1) * 128],
                            xfm[:, dc * TC + tl * 512: dc * TC + (tl + 1) * 512],
                            start=(dc == 0), stop=(dc == 7)),
                            inc=("T" if dc == 7 else None))
                    t_h3t[tl] = S.count("T")
                for k, tl in enumerate(TO):
                    b = H3B[k]
                    o = hbase + ic * TC + tl * 512
                    S.wait("vector", "T", t_h3t[tl])
                    S.wait("vector", "C", c_silut[tl])
                    S.emit("vector", lambda e, tl=tl, o=o, b=b: e.tensor_tensor(
                        out=big0[:, o:o + 512], in0=P[b][:],
                        in1=kfm[:, tl * 512:(tl + 1) * 512], op=OP.mult), inc="V")
                    lastV[b] = S.count("V")
                lastVkfm[0] = S.count("V")
                if not shared:
                    for tl in TO:
                        o = hbase + ic * TC + tl * 512
                        S.emit("vector", lambda e, tl=tl, o=o, cb=cbp[tl]: e.tensor_tensor(
                            out=big0[:, o:o + 512], in0=big0[:, o:o + 512],
                            in1=cb, op=OP.mult),
                            inc=("V" if tl == TO[-1] else None))
            t_h3_done[idx] = S.count("T")
            slot_free[s1n] = t_h3_done[idx]
            slot_free[s3n] = t_h3_done[idx]

        YB = [4, 5, 6, 0, 1, 2, 3]

        def ffn_w2(members, v_ready, fuse_ln2=False):
            nmem = len(members)
            S.wait("tensor", "V", v_ready)
            for idx, _ in members:
                S.wait("tensor", "DG", dg2[idx])
            for i in range(NCH):
                b0 = YB[(2 * i) % 7]
                b1 = YB[(2 * i + 1) % 7]
                wait_bank(b0)
                wait_bank(b1)
                for mi, (idx, slot) in enumerate(members):
                    w2s = SLOT[TRIPLE[idx % 2][2]]
                    hbase = slot * 8192
                    for ic in range(4):
                        for half, b in ((0, b0), (1, b1)):
                            first = (mi == 0 and ic == 0)
                            last = (mi == nmem - 1 and ic == 3)
                            S.emit("tensor", lambda e, w2s=w2s, hbase=hbase, i=i, ic=ic,
                                   half=half, b=b, first=first, last=last: e.matmul(
                                P[b][:],
                                big0[:, hbase + ic * TC + i * 128: hbase + ic * TC + (i + 1) * 128],
                                w2s[:, ic * 1024 + half * 512: ic * 1024 + (half + 1) * 512],
                                start=first, stop=last),
                                inc=("T" if (last and half == 1) else None))
                t_y = S.count("T")
                S.wait("vector", "T", t_y)
                for half, b in ((0, b0), (1, b1)):
                    if fuse_ln2:
                        st_dst = stA if half == 0 else stT
                        S.emit("vector", lambda e, i=i, half=half, b=b, st_dst=st_dst: e.scalar_tensor_tensor(
                            out=resid[:, i * 1024 + half * 512: i * 1024 + half * 512 + 512],
                            in0=P[b][:], scalar=1.0,
                            in1=resid[:, i * 1024 + half * 512: i * 1024 + half * 512 + 512],
                            op0=OP.mult, op1=OP.add,
                            accum_out=st_dst[:, i:i + 1]), inc=("V" if half == 1 else None))
                    else:
                        S.emit("vector", lambda e, i=i, half=half, b=b: e.tensor_tensor(
                            out=resid[:, i * 1024 + half * 512: i * 1024 + half * 512 + 512],
                            in0=P[b][:],
                            in1=resid[:, i * 1024 + half * 512: i * 1024 + half * 512 + 512],
                            op=OP.add), inc=("V" if half == 1 else None))
                vv = S.count("V")
                lastV[b0] = vv
                lastV[b1] = vv
                if fuse_ln2:
                    sq0 = YB[(2 * i + 4) % 7]
                    sq1 = YB[(2 * i + 5) % 7]
                    S.wait("scalar", "V", vv)
                    S.emit("scalar", lambda e, i=i, sq0=sq0: e.activation(
                        out=P[sq0][:], in_=resid[:, i * 1024: i * 1024 + 512], func=AF.Square,
                        accum_out=stB[:, i:i + 1]))
                    S.emit("scalar", lambda e, i=i, sq1=sq1: e.activation(
                        out=P[sq1][:], in_=resid[:, i * 1024 + 512:(i + 1) * 1024], func=AF.Square,
                        accum_out=stC[:, i:i + 1]), inc="C")
                    lastC[sq0] = S.count("C")
                    lastC[sq1] = S.count("C")
            t_w2 = S.count("T")
            for idx, _ in members:
                slot_free[TRIPLE[idx % 2][2]] = t_w2
            return t_w2

        S.wait("tensor", "V", v_combfm)
        for pi in range(8):
            e0, e1 = 2 * pi, 2 * pi + 1
            ffn_expert(e0, 0)
            ffn_expert(e1, 1)
            if pi < 7:
                issue_dma13(2 * (pi + 1))
                issue_dma13(2 * (pi + 1) + 1)
            elif pi == 7:
                issue_dma13(16)
            t_w2 = ffn_w2([(e0, 0), (e1, 1)], S.count("V"))
            if pi < 7:
                issue_dma2(2 * (pi + 1))
                issue_dma2(2 * (pi + 1) + 1)
            else:
                issue_dma2(16)
        ffn_expert(16, 0)
        ffn_w2([(16, 0)], S.count("V"), fuse_ln2=True)
        v_r2 = S.count("V")
        c_ln2sq = S.count("C")

        S.wait("vector", "C", c_ln2sq)
        ln_smalls()
        S.emit("vector", lambda e: e.scalar_tensor_tensor(
            out=stT[:], in0=stM[:], scalar=-1.0, op0=OP.mult,
            in1=stR[:], op1=OP.mult), inc="V")
        S.wait("scalar", "V", S.count("V"))
        for i in range(NCH):
            if i % 2 == 0:
                S.emit("vector", lambda e, i=i: e.tensor_scalar(
                    out=resid[:, i * 1024:(i + 1) * 1024],
                    in0=resid[:, i * 1024:(i + 1) * 1024],
                    scalar1=stM[:, i:i + 1], scalar2=stR[:, i:i + 1],
                    op0=OP.subtract, op1=OP.mult), inc=("V" if i % 4 == 2 else None))
            else:
                S.emit("scalar", lambda e, i=i: e.activation(
                    out=resid[:, i * 1024:(i + 1) * 1024],
                    in_=resid[:, i * 1024:(i + 1) * 1024],
                    func=AF.Identity, bias=stT[:, i:i + 1],
                    scale=stR[:, i:i + 1]), inc=("C" if i % 4 == 3 else None))
            if i % 4 == 3:
                g = i // 4
                S.wait("sync", "V", S.count("V"))
                S.wait("sync", "C", S.count("C"))
                S.emit("sync", lambda e, g=g: e.dma_start(
                    out=out[g * 512:(g + 1) * 512, :].rearrange("(i p) d -> p i d", p=128),
                    in_=resid[:, g * 4096:(g + 1) * 4096].rearrange("p (i d) -> p i d", d=D)),
                    inc="DO", k=16)
        S.wait("sync", "DO")
        with nc.allow_low_precision(reason="bf16 transposes batched into one PSUM bank write disjoint column slots; no cross-slot accumulation occurs"):
            S.run(block)
    return nc


def kernel(**inputs):
    x = np.asarray(inputs["L0_attn_x"], dtype=np.float32).reshape(4 * 4096, D)
    f = np.float32
    wqv = np.ascontiguousarray(inputs["L0_attn_Wq"], dtype=f)
    wkv = np.ascontiguousarray(inputs["L0_attn_Wk"], dtype=f)
    wvv = np.ascontiguousarray(inputs["L0_attn_Wv"], dtype=f)
    proj = np.ascontiguousarray(inputs["L0_attn_proj"], dtype=f)
    wov = np.ascontiguousarray(inputs["L0_attn_Wo"], dtype=f)
    bov = np.ascontiguousarray(inputs["L0_attn_bo"], dtype=f).reshape(1, D)
    gwtv = np.ascontiguousarray(np.asarray(inputs["L0_ffn_gate_w"], dtype=f).T)
    gbv = np.ascontiguousarray(
        np.broadcast_to(np.asarray(inputs["L0_ffn_gate_bias"], dtype=f).reshape(1, E), (128, E)))
    w1tv = np.ascontiguousarray(np.asarray(inputs["L0_ffn_w1"], dtype=f).transpose(0, 2, 1))
    w3tv = np.ascontiguousarray(np.asarray(inputs["L0_ffn_w3"], dtype=f).transpose(0, 2, 1))
    w2tv = np.ascontiguousarray(np.asarray(inputs["L0_ffn_w2"], dtype=f).transpose(0, 2, 1))
    ws1tv = np.ascontiguousarray(np.asarray(inputs["L0_ffn_ws1"], dtype=f).T)
    ws3tv = np.ascontiguousarray(np.asarray(inputs["L0_ffn_ws3"], dtype=f).T)
    ws2tv = np.ascontiguousarray(np.asarray(inputs["L0_ffn_ws2"], dtype=f).T)

    identv = np.eye(128, dtype=ml_dtypes.bfloat16)
    onehv = np.zeros((16, 2048), dtype=ml_dtypes.bfloat16)
    for e_ in range(16):
        onehv[e_, e_ * 128:(e_ + 1) * 128] = 1
    nc = build_nc()
    in_maps = []
    for c in range(N_CORES):
        b, h = c // 2, c % 2
        own = np.ascontiguousarray(x[b * 4096 + h * TC: b * 4096 + (h + 1) * TC])
        in_maps.append({
            "x_own": own,
            "wq": wqv, "wk": wkv, "wv": wvv, "proj": proj, "wo": wov, "bo": bov,
            "gwt": gwtv, "gbias": gbv,
            "w1t": w1tv, "w3t": w3tv, "w2t": w2tv,
            "ws1t": ws1tv, "ws3t": ws3tv, "ws2t": ws2tv,
            "identc": identv, "onehc": onehv,
        })
    res = run_bass_kernel_spmd(nc, in_maps, core_ids=list(range(N_CORES)))
    outs = [np.asarray(res.results[c]["out"]) for c in range(N_CORES)]
    return np.concatenate(outs, axis=0).reshape(4, 4096, D).astype(np.float32)


# version 16
# speedup vs baseline: 1.0693x; 1.0693x over previous
import numpy as np
import ml_dtypes
import concourse.bass as bass
import concourse.mybir as mybir
from concourse.bass_utils import run_bass_kernel_spmd
from concourse.alu_op_type import AluOpType as OP

F32 = mybir.dt.float32
BF16 = mybir.dt.bfloat16
AF = mybir.ActivationFunctionType
AX = mybir.AxisListType

DEBUG = False
DEBUG_XFM = False
WARMUP = False
KVW = 129
AMPUTATE = False
SNAPDUMP = False
N_CORES = 8
TC = 2048
NCH = 16
D = 1024
E = 16
INTER = 512
EPS_A = 1e-6
EPS_LN = 1e-5
BIG = 1.0e9


class Sched:

    ENGINES = ("tensor", "vector", "scalar", "gpsimd", "sync")

    def __init__(self, nc):
        self.nc = nc
        self.ops = {e: [] for e in self.ENGINES}
        self.sems = {}

    def sem(self, name, handle):
        self.sems[name] = [handle, 0]

    def emit(self, eng, fn, inc=None, k=1):
        if inc is not None:
            h = self.sems[inc][0]
            self.sems[inc][1] += k
            self.ops[eng].append(lambda e, fn=fn, h=h, k=k: fn(e).then_inc(h, k))
        else:
            self.ops[eng].append(fn)

    def wait(self, eng, sem, val=None):
        h, cnt = self.sems[sem]
        v = cnt if val is None else val
        if v and v > 0:
            self.ops[eng].append(lambda e, h=h, v=v: e.wait_ge(h, v))

    def count(self, sem):
        return self.sems[sem][1]

    def drain(self, eng):
        self.ops[eng].append(lambda e: e.drain())

    def run(self, block):
        for engname in self.ENGINES:
            def body(eng, engname=engname):
                for fn in self.ops[engname]:
                    fn(eng)
            getattr(block, engname)(body)


def build_nc():
    nc = bass.Bass("TRN2", target_bir_lowering=False, debug=False)

    xo = nc.dram_tensor("x_own", [TC, D], F32, kind="ExternalInput").ap()
    wq = nc.dram_tensor("wq", [D, D], F32, kind="ExternalInput").ap()
    wk = nc.dram_tensor("wk", [D, D], F32, kind="ExternalInput").ap()
    wv = nc.dram_tensor("wv", [D, D], F32, kind="ExternalInput").ap()
    pj = nc.dram_tensor("proj", [64, 64], F32, kind="ExternalInput").ap()
    wo = nc.dram_tensor("wo", [D, D], F32, kind="ExternalInput").ap()
    bo = nc.dram_tensor("bo", [1, D], F32, kind="ExternalInput").ap()
    gwt = nc.dram_tensor("gwt", [D, E], F32, kind="ExternalInput").ap()
    gbias = nc.dram_tensor("gbias", [128, E], F32, kind="ExternalInput").ap()
    identc = nc.dram_tensor("identc", [128, 128], BF16, kind="ExternalInput").ap()
    onehc = nc.dram_tensor("onehc", [16, 2048], BF16, kind="ExternalInput").ap()
    w1t = nc.dram_tensor("w1t", [E, D, INTER], F32, kind="ExternalInput").ap()
    w3t = nc.dram_tensor("w3t", [E, D, INTER], F32, kind="ExternalInput").ap()
    w2t = nc.dram_tensor("w2t", [E, INTER, D], F32, kind="ExternalInput").ap()
    ws1t = nc.dram_tensor("ws1t", [D, INTER], F32, kind="ExternalInput").ap()
    ws3t = nc.dram_tensor("ws3t", [D, INTER], F32, kind="ExternalInput").ap()
    ws2t = nc.dram_tensor("ws2t", [INTER, D], F32, kind="ExternalInput").ap()
    out = nc.dram_tensor("out", [TC, D], F32, kind="ExternalOutput").ap()
    kv_ci = nc.dram_tensor("kv_ci", [128, 1032], BF16, kind="Internal").ap()
    kv_co = nc.dram_tensor("kv_co", [128, 1032], BF16, kind="Internal", addr_space="Local").ap()
    dbg = {}
    if DEBUG:
        dbg["kvblk"] = nc.dram_tensor("d_kvblk", [128, 1024], BF16, kind="ExternalOutput").ap()
        dbg["ksel"] = nc.dram_tensor("d_ksel", [128, 145], BF16, kind="ExternalOutput").ap()
        dbg["kvstage"] = nc.dram_tensor("d_kvstage", [128, 1032], BF16, kind="ExternalOutput").ap()
        dbg["kvparts"] = nc.dram_tensor("d_kvparts", [128, 15 * 1032], BF16, kind="ExternalOutput").ap()
        dbg["kvp456"] = nc.dram_tensor("d_kvp456", [128, 1536], BF16, kind="ExternalOutput").ap()
        dbg["kvred"] = nc.dram_tensor("d_kvred", [128, 1536], BF16, kind="ExternalOutput").ap()
        dbg["outfm"] = nc.dram_tensor("d_outfm", [128, 16384], BF16, kind="ExternalOutput").ap()
        dbg["x1"] = nc.dram_tensor("d_x1", [TC, D], F32, kind="ExternalOutput").ap()
        dbg["scores"] = nc.dram_tensor("d_scores", [128, NCH * 16], F32, kind="ExternalOutput").ap()
        dbg["comb"] = nc.dram_tensor("d_comb", [128, NCH * 16], BF16, kind="ExternalOutput").ap()
        dbg["xfm"] = nc.dram_tensor("d_xfm", [128, 16384], BF16, kind="ExternalOutput").ap()
        dbg["kfm"] = nc.dram_tensor("d_kfm", [128, 4096], BF16, kind="ExternalOutput").ap()
        dbg["vext"] = nc.dram_tensor("d_vext", [128, 1056], BF16, kind="ExternalOutput").ap()
        dbg["kftm"] = nc.dram_tensor("d_kftm", [128, 1024], BF16, kind="ExternalOutput").ap()
        dbg["p4raw"] = nc.dram_tensor("d_p4raw", [128, 512], F32, kind="ExternalOutput").ap()

    from contextlib import ExitStack
    with ExitStack() as ctx:
        resid = ctx.enter_context(nc.sbuf_tensor("resid", [128, NCH * D], F32))
        xcast = ctx.enter_context(nc.sbuf_tensor("xcast", [128, 2 * D], BF16))
        xfm = ctx.enter_context(nc.sbuf_tensor("xfm", [128, 8 * TC], BF16))
        wA = ctx.enter_context(nc.sbuf_tensor("wA", [128, 12288], BF16))
        wB = ctx.enter_context(nc.sbuf_tensor("wB", [128, 12288], BF16))
        big0 = ctx.enter_context(nc.sbuf_tensor("big0", [128, 16384], BF16))
        vext = ctx.enter_context(nc.sbuf_tensor("vext", [128, 8 * 129], BF16))
        kftm = ctx.enter_context(nc.sbuf_tensor("kftm", [128, 1024], BF16))
        kfm = ctx.enter_context(nc.sbuf_tensor("kfm", [128, 8 * 512], BF16))
        kvblk = ctx.enter_context(nc.sbuf_tensor("kvblk", [128, 8 * 128], BF16))
        ksel = ctx.enter_context(nc.sbuf_tensor("ksel", [128, 145], BF16))
        scorebuf = ctx.enter_context(nc.sbuf_tensor("scorebuf", [128, NCH * 16], F32))
        combtm = ctx.enter_context(nc.sbuf_tensor("combtm", [128, NCH * 16], BF16))
        gbtile = ctx.enter_context(nc.sbuf_tensor("gbtile", [128, E], F32))
        gwtb = ctx.enter_context(nc.sbuf_tensor("gwtb", [128, 8 * 16], BF16))
        aux16 = ctx.enter_context(nc.sbuf_tensor("aux16", [16, TC], BF16))
        gscb = aux16
        rden = aux16
        combfm = aux16
        projblk = ctx.enter_context(nc.sbuf_tensor("projblk", [128, 128], BF16))
        ident = ctx.enter_context(nc.sbuf_tensor("ident", [128, 128], BF16))
        oneh = ctx.enter_context(nc.sbuf_tensor("oneh", [16, 2048], BF16))
        onecol = ctx.enter_context(nc.sbuf_tensor("onecol", [128, 1], BF16))
        borow = kftm
        stA = ctx.enter_context(nc.sbuf_tensor("stA", [128, 16], F32))
        stB = ctx.enter_context(nc.sbuf_tensor("stB", [128, 16], F32))
        stC = ctx.enter_context(nc.sbuf_tensor("stC", [128, 16], F32))
        stM = ctx.enter_context(nc.sbuf_tensor("stM", [128, 16], F32))
        stR = ctx.enter_context(nc.sbuf_tensor("stR", [128, 16], F32))
        stT = ctx.enter_context(nc.sbuf_tensor("stT", [128, 16], F32))
        stU = ctx.enter_context(nc.sbuf_tensor("stU", [128, 16], F32))
        stV2 = ctx.enter_context(nc.sbuf_tensor("stV2", [128, 16], F32))
        rtA = ctx.enter_context(nc.sbuf_tensor("rtA", [128, 88], F32))
        rtB = ctx.enter_context(nc.sbuf_tensor("rtB", [128, 88], F32))
        rt_g, rt_g2, rt_g8, rt_m8 = rtA[:, 0:4], rtA[:, 4:8], rtA[:, 8:16], rtA[:, 16:24]
        rt_keep, rt_t, rt_sm, rt_sel = rtA[:, 24:40], rtA[:, 40:56], rtA[:, 56:72], rtA[:, 72:88]
        rt2_g, rt2_g2, rt2_g8, rt2_m8 = rtB[:, 0:4], rtB[:, 4:8], rtB[:, 8:16], rtB[:, 16:24]
        rt2_keep, rt2_t, rt2_sm, rt2_sel = rtB[:, 24:40], rtB[:, 40:56], rtB[:, 56:72], rtB[:, 72:88]
        p0 = ctx.enter_context(nc.psum_tensor("p0", [128, 512], F32))
        p1 = ctx.enter_context(nc.psum_tensor("p1", [128, 512], F32))
        p2 = ctx.enter_context(nc.psum_tensor("p2", [128, 512], F32))
        p3 = ctx.enter_context(nc.psum_tensor("p3", [128, 512], F32))
        p4 = ctx.enter_context(nc.psum_tensor("p4", [128, 512], F32))
        p5 = ctx.enter_context(nc.psum_tensor("p5", [128, 512], F32))
        p6 = ctx.enter_context(nc.psum_tensor("p6", [128, 512], F32))
        pT = ctx.enter_context(nc.psum_tensor("pT", [128, 512], F32))
        sDG = ctx.enter_context(nc.semaphore("DG"))
        sDS = ctx.enter_context(nc.semaphore("DS"))
        sDO = ctx.enter_context(nc.semaphore("DO"))
        sT = ctx.enter_context(nc.semaphore("T"))
        sV = ctx.enter_context(nc.semaphore("V"))
        sC = ctx.enter_context(nc.semaphore("C"))
        sG = ctx.enter_context(nc.semaphore("G"))
        block = ctx.enter_context(nc.Block())
        S = Sched(nc)
        for nm, h in (("DG", sDG), ("DS", sDS), ("DO", sDO), ("T", sT),
                      ("V", sV), ("C", sC), ("G", sG)):
            S.sem(nm, h)
        P = [p0, p1, p2, p3, p4, p5, p6]
        lastV = {b: 0 for b in range(7)}
        lastC = {b: 0 for b in range(7)}

        def wait_bank(b):
            if lastV[b]:
                S.wait("tensor", "V", lastV[b])
            if lastC[b]:
                S.wait("tensor", "C", lastC[b])

        S.emit("sync", lambda e: e.dma_start(out=gbtile[:], in_=gbias), inc="DS", k=16)
        S.emit("sync", lambda e: e.dma_start(out=ident[:], in_=identc), inc="DS", k=16)
        S.emit("sync", lambda e: e.dma_start(out=oneh[:], in_=onehc), inc="DS", k=16)
        for q in range(16):
            S.emit("sync", lambda e, q=q: e.dma_start(
                out=resid[:, q * 1024:(q + 1) * 1024],
                in_=xo[q * 128:(q + 1) * 128, :]), inc="DS", k=16)

        S.emit("gpsimd", lambda e: e.dma_start(out=projblk[0:64, 0:64], in_=pj), inc="DG", k=16)
        S.emit("gpsimd", lambda e: e.dma_start(out=projblk[64:128, 64:128], in_=pj), inc="DG", k=16)
        S.emit("gpsimd", lambda e: e.dma_start(
            out=gwtb[:].rearrange("p (c j) -> p c j", j=E),
            in_=gwt.rearrange("(c p) j -> p c j", p=128)), inc="DG", k=16)
        dg_small = S.count("DG")
        S.emit("gpsimd", lambda e: e.dma_start(
            out=wA[:, 0:8192].rearrange("p (c j) -> p c j", j=D),
            in_=wk.rearrange("(c p) j -> p c j", p=128)), inc="DG", k=16)
        dg_wk = S.count("DG")
        S.emit("gpsimd", lambda e: e.dma_start(
            out=wB[:, 0:8192].rearrange("p (c j) -> p c j", j=D),
            in_=wv.rearrange("(c p) j -> p c j", p=128)), inc="DG", k=16)
        dg_wv = S.count("DG")
        S.emit("gpsimd", lambda e: e.dma_start(
            out=wA[:, 8192:12288].rearrange("p (c j) -> p c j", j=512),
            in_=wq[:, 0:512].rearrange("(c p) j -> p c j", p=128)), inc="DG", k=16)
        S.emit("gpsimd", lambda e: e.dma_start(
            out=wB[:, 8192:12288].rearrange("p (c j) -> p c j", j=512),
            in_=wq[:, 512:1024].rearrange("(c p) j -> p c j", p=128)), inc="DG", k=16)
        dg_w0 = S.count("DG")

        S.wait("vector", "DS", 48)
        S.emit("vector", lambda e: e.memset(onecol[:], 1.0))
        S.wait("vector", "DG", dg_small)
        S.emit("vector", lambda e: e.memset(projblk[0:64, 64:128], 0.0))
        S.emit("vector", lambda e: e.memset(projblk[64:128, 0:64], 0.0))
        S.emit("vector", lambda e: e.memset(kvblk[:], 0.0))
        S.emit("vector", lambda e: e.memset(ksel[:], 0.0))
        S.emit("vector", lambda e: e.memset(rtA[:, 12:16], -BIG))
        S.emit("vector", lambda e: e.memset(rtB[:, 12:16], -BIG))
        identf = scorebuf[:, 0:128]
        S.emit("vector", lambda e: e.tensor_copy(out=identf, in_=ident[:]))
        for c in range(8):
            S.emit("vector", lambda e, c=c: e.memset(
                vext[:, c * 129 + 128:c * 129 + 129], 1.0))
        for cs in range(7):
            S.emit("vector", lambda e, cs=cs: e.memset(
                xcast[:, 1024 + cs * 129 + 128:1024 + cs * 129 + 129], 1.0))
        S.emit("vector", lambda e: e.memset(ksel[:, 128:129], 1.0), inc="V")
        v_consts = S.count("V")

        t_xtr = [0] * NCH
        v_xfm_chunk = [0] * NCH
        v_pT = 0
        for i in range(NCH):
            if i == 0:
                S.wait("tensor", "V", v_consts)
                for bk in (pT, p6):
                    for c in range(4):
                        S.emit("tensor", lambda e, bk=bk, c=c: e.matmul(
                            bk[:, c * 128:(c + 1) * 128], identf, identf,
                            is_transpose=True, start=(c == 0), stop=(c == 3)))
            S.wait("tensor", "DS", 48 + 16 * (i + 1))
            for half in range(2):
                if v_pT:
                    S.wait("tensor", "V", v_pT)
                for c4 in range(4):
                    c = half * 4 + c4
                    S.emit("tensor", lambda e, i=i, c=c, c4=c4: e.matmul(
                        p6[:, c4 * 128:(c4 + 1) * 128],
                        resid[:, i * 1024 + c * 128:i * 1024 + (c + 1) * 128],
                        identf, is_transpose=True, start=(c4 == 0), stop=(c4 == 3)),
                        inc=("T" if c4 == 3 else None))
                tb = S.count("T")
                S.wait("vector", "T", tb)
                S.emit("vector", lambda e, i=i, half=half: e.tensor_copy(
                    out=xfm[:].rearrange("p (c t) -> p c t", t=TC)
                        [:, half * 4:(half + 1) * 4, i * 128:(i + 1) * 128],
                    in_=p6[:].rearrange("p (c t) -> p c t", t=128)), inc="V")
                v_pT = S.count("V")
            t_xtr[i] = S.count("T")
            v_xfm_chunk[i] = S.count("V")
        v_xfm_done = S.count("V")
        if DEBUG:
            S.wait("sync", "V", v_xfm_done)
            if DEBUG_XFM:
                S.emit("sync", lambda e: e.dma_start(out=dbg["xfm"], in_=xfm[:]), inc="DO", k=16)

        kv_bank = {0: 4, 1: 4, 2: 4, 3: 5, 4: 5, 5: 5, 6: 6, 7: 6}
        kv_off = {0: 0, 1: 170, 2: 340, 3: 0, 4: 170, 5: 340, 6: 0, 7: 170}
        kv_first = {4: 0, 5: 3, 6: 6}
        kv_last = {4: 2, 5: 5, 6: 7}

        t_kv_last = 0
        t_kf_last = 0
        kv_pend = [None]
        odd_ones = [False]
        t_kv_by_par = {0: 0, 1: 0}
        v_kvevac = [0]

        def kf_buf(par):
            return kftm if par == 0 else xcast[:, 0:1024]

        def emit_kv(pend):
            nonlocal t_kv_last
            ptile, ptc, v_ready = pend
            gtc = ptile * 4 + ptc
            par = ptc % 2
            S.wait("tensor", "V", v_ready)
            if v_kvevac[0]:
                S.wait("tensor", "V", v_kvevac[0])
            kb = kf_buf(par)
            vsrc = (lambda c: vext[:, c * 129:c * 129 + 129]) if par == 0 else \
                   (lambda c: xcast[:, 1024 + c * 129:1024 + c * 129 + 129]
                    if c < 7 else ksel[:, 0:129])
            for c in range(8):
                first = (kv_first[kv_bank[c]] == c)
                last = (kv_last[kv_bank[c]] == c)
                S.emit("tensor", lambda e, c=c, kb=kb, vs=vsrc(c), first=first,
                       last=last: e.matmul(
                    P[kv_bank[c]][:, kv_off[c]:kv_off[c] + KVW],
                    kb[:, c * 128:(c + 1) * 128], vs[:, 0:KVW],
                    start=first, stop=last, skip_group_check=True),
                    inc=("T" if c == 7 else None))
            t_kv_last = S.count("T")
            t_kv_by_par[par] = t_kv_last
            if gtc < 15:
                S.wait("vector", "T", t_kv_last)
                for b, nj in ((4, 3), (5, 3), (6, 2)):
                    base = {4: 0, 5: 3, 6: 6}[b]
                    S.emit("vector", lambda e, gtc=gtc, b=b, nj=nj, base=base: e.tensor_copy(
                        out=big0[:, gtc * 1032 + base * 129: gtc * 1032 + (base + nj) * 129]
                            .rearrange("p (j k) -> p j k", k=129),
                        in_=P[b][:, 0:nj * 170]
                            .rearrange("p (j k) -> p j k", k=170)[:, :, 0:129]),
                        inc=("V" if b == 6 else None))
                v_kvevac[0] = S.count("V")
                for b in (4, 5, 6):
                    lastV[b] = v_kvevac[0]
                if DEBUG and AMPUTATE and SNAPDUMP and gtc == 0:
                    S.wait("sync", "V", v_kvevac[0])
                    S.emit("sync", lambda e: e.dma_start(
                        out=dbg["kftm"], in_=kftm[:]), inc="DO", k=16)
                    S.emit("sync", lambda e: e.dma_start(
                        out=dbg["vext"][:, 0:1032], in_=vext[:]), inc="DO", k=16)
                    S.emit("sync", lambda e: e.dma_start(
                        out=dbg["outfm"][:, 0:1024], in_=kftm[:]), inc="DO", k=16)
                    S.emit("sync", lambda e: e.dma_start(
                        out=dbg["outfm"][:, 1024:2056], in_=vext[:]), inc="DO", k=16)
                    S.wait("vector", "DO", S.count("DO"))
        S.wait("tensor", "DG", dg_wk)
        for bknum in (4, 5, 6):
            S.emit("tensor", lambda e, bknum=bknum: e.matmul(
                P[bknum][:, 0:512], wA[:, 0:128], wA[:, 0:512],
                start=True, stop=True))
        if WARMUP:
            for r in range(WARMUP if WARMUP > 1 else 15):
                for dc in range(8):
                    S.emit("tensor", lambda e, dc=dc: e.matmul(
                        P[0][:], wA[:, dc * 1024:dc * 1024 + 128], wA[:, 0:512],
                        start=(dc == 0), stop=(dc == 7)))

        for tile in range(4):
            own = True
            buf = tile % 2
            S.wait("tensor", "V", v_xfm_chunk[4 * tile + 3])
            if kv_pend[0] is not None:
                emit_kv(kv_pend[0])
                kv_pend[0] = None

            def rhs512(dc, tile=tile):
                return xfm[:, dc * TC + tile * 512: dc * TC + tile * 512 + 512]

            def blk128(dc, tc, tile=tile):
                o = dc * TC + tile * 512 + tc * 128
                return xfm[:, o:o + 128]

            for c in range(8):
                bank = c % 2
                wait_bank(bank)
                for dc in range(8):
                    rhs = rhs512(dc)
                    S.emit("tensor", lambda e, bank=bank, dc=dc, c=c, rhs=rhs: e.matmul(
                        P[bank][:], wA[:, dc * 1024 + c * 128: dc * 1024 + (c + 1) * 128],
                        rhs, start=(dc == 0), stop=(dc == 7)),
                        inc=("T" if dc == 7 else None))
                t_k = S.count("T")
                S.wait("vector", "T", t_k)
                if t_kf_last:
                    S.wait("vector", "T", t_kf_last)
                S.emit("vector", lambda e, bank=bank, c=c: e.tensor_copy(
                    out=kfm[:, c * 512:(c + 1) * 512], in_=P[bank][:]), inc="V")
                lastV[bank] = S.count("V")

            for tc in range(4):
                wait_bank(0)
                wait_bank(1)
                if kv_pend[0] is not None:
                    pass
                for c in range(8):
                    bank = 0 if c < 4 else 1
                    S.emit("tensor", lambda e, c=c, bank=bank, tc=tc: e.matmul(
                        P[bank][:, (c % 4) * 128:((c % 4) + 1) * 128],
                        kfm[:, c * 512 + tc * 128: c * 512 + tc * 128 + 128],
                        projblk[:], start=(c % 4 == 0), stop=(c % 4 == 3)),
                        inc=("T" if c % 4 == 3 else None))
                t_kf_last = S.count("T")

                if tile == 0 and tc == 0:
                    S.wait("tensor", "DG", dg_wv)
                wait_bank(2)
                wait_bank(3)
                for dc in range(8):
                    blk = blk128(dc, tc)
                    for half, bank in ((0, 2), (1, 3)):
                        S.emit("tensor", lambda e, blk=blk, dc=dc, half=half, bank=bank: e.matmul(
                            P[bank][:], blk,
                            wB[:, dc * 1024 + half * 512: dc * 1024 + (half + 1) * 512],
                            start=(dc == 0), stop=(dc == 7)),
                            inc=("T" if (dc == 7 and half == 1) else None))
                t_v = S.count("T")

                if kv_pend[0] is not None:
                    emit_kv(kv_pend[0])


                par = tc % 2
                kb = kf_buf(par)
                S.wait("vector", "T", t_kf_last)
                if t_kv_by_par[par]:
                    S.wait("vector", "T", t_kv_by_par[par])
                for bank in (0, 1):
                    S.emit("vector", lambda e, kb=kb, bank=bank: e.tensor_scalar(
                        out=kb[:, bank * 512:(bank + 1) * 512], in0=P[bank][:],
                        scalar1=EPS_A, scalar2=EPS_A, op0=OP.add, op1=OP.max), inc="V")
                    lastV[bank] = S.count("V")
                S.wait("vector", "T", t_v)
                if par == 0:
                    for half, bank in ((0, 2), (1, 3)):
                        S.emit("vector", lambda e, half=half, bank=bank: e.tensor_copy(
                            out=vext[:].rearrange("p (s j) -> p s j", j=129)
                                [:, half * 4:(half + 1) * 4, 0:128],
                            in_=P[bank][:].rearrange("p (s j) -> p s j", j=128)), inc="V")
                        lastV[bank] = S.count("V")
                else:
                    S.emit("vector", lambda e: e.tensor_copy(
                        out=xcast[:, 1024:1024 + 4 * 129]
                            .rearrange("p (s j) -> p s j", j=129)[:, :, 0:128],
                        in_=P[2][:].rearrange("p (s j) -> p s j", j=128)), inc="V")
                    lastV[2] = S.count("V")
                    S.emit("vector", lambda e: e.tensor_copy(
                        out=xcast[:, 1024 + 4 * 129:1024 + 7 * 129]
                            .rearrange("p (s j) -> p s j", j=129)[:, :, 0:128],
                        in_=P[3][:, 0:384].rearrange("p (s j) -> p s j", j=128)), inc="V")
                    S.emit("vector", lambda e: e.tensor_copy(
                        out=ksel[:, 0:128], in_=P[3][:, 384:512]), inc="V")
                    lastV[3] = S.count("V")
                v_now = S.count("V")
                kv_pend[0] = (tile, tc, v_now)

        if kv_pend[0] is not None:
            emit_kv(kv_pend[0])
            kv_pend[0] = None
        t_ph1 = S.count("T")

        S.wait("vector", "T", t_ph1)
        S.emit("vector", lambda e: e.memset(ksel[:, 0:128], 0.0))
        if DEBUG:
            for b in (4, 5, 6):
                S.emit("vector", lambda e, b=b: e.tensor_copy(
                    out=kfm[:, (b - 4) * 512:(b - 4) * 512 + 510],
                    in_=P[b][:, 0:510]), inc="V")
            S.wait("sync", "V", S.count("V"))
            S.emit("sync", lambda e: e.dma_start(
                out=dbg["kvparts"], in_=big0[:, 0:15 * 1032]), inc="DO", k=16)
            S.emit("sync", lambda e: e.dma_start(
                out=dbg["kvp456"][:, 0:1536], in_=kfm[:, 0:1536]), inc="DO", k=16)
            S.wait("vector", "DO", S.count("DO"))
        if AMPUTATE:
            S.wait("sync", "DO")
            with nc.allow_low_precision(reason="debug amputated build"):
                S.run(block)
            return nc
        for lv_step in ((1, 2), (2, 4), (4, 8), (8, 16)):
            d, stride = lv_step
            for a0 in range(0, 15, stride):
                b0 = a0 + d
                if b0 > 14:
                    continue
                S.emit("vector", lambda e, a0=a0, b0=b0: e.tensor_tensor(
                    out=big0[:, a0 * 1032:(a0 + 1) * 1032],
                    in0=big0[:, a0 * 1032:(a0 + 1) * 1032],
                    in1=big0[:, b0 * 1032:(b0 + 1) * 1032], op=OP.add))
        S.drain("vector")
        for b, nj in ((4, 3), (5, 3), (6, 2)):
            base = {4: 0, 5: 3, 6: 6}[b]
            S.emit("vector", lambda e, b=b, nj=nj, base=base: e.tensor_tensor(
                out=big0[:, base * 129:(base + nj) * 129].rearrange("p (j k) -> p j k", k=129),
                in0=P[b][:, 0:nj * 170].rearrange("p (j k) -> p j k", k=170)[:, :, 0:129],
                in1=big0[:, base * 129:(base + nj) * 129].rearrange("p (j k) -> p j k", k=129),
                op=OP.add), inc=("V" if b == 6 else None))
        v_kvstage = S.count("V")
        for b in (4, 5, 6):
            lastV[b] = v_kvstage

        S.wait("gpsimd", "V", v_kvstage)
        S.emit("gpsimd", lambda e: e.dma_start(out=kv_ci, in_=big0[:, 0:1032]), inc="DG", k=16)
        dg_ci = S.count("DG")
        S.wait("gpsimd", "DG", dg_ci)
        S.emit("gpsimd", lambda e: e.collective_compute(
            "AllReduce", mybir.AluOpType.add, ins=[kv_ci], outs=[kv_co],
            replica_groups=[[0, 1], [2, 3], [4, 5], [6, 7]]), inc="G")
        g_cc = S.count("G")
        S.wait("gpsimd", "G", g_cc)
        S.emit("gpsimd", lambda e: e.dma_start(
            out=kvblk[0:64, 0:1024].rearrange("p (j k) -> p j k", k=128)[:, :, 0:64],
            in_=kv_co[0:64, 0:1032].rearrange("p (j k) -> p j k", k=129)[:, :, 0:64]),
            inc="DG", k=16)
        S.emit("gpsimd", lambda e: e.dma_start(
            out=kvblk[64:128, 0:1024].rearrange("p (j k) -> p j k", k=128)[:, :, 64:128],
            in_=kv_co[64:128, 0:1032].rearrange("p (j k) -> p j k", k=129)[:, :, 64:128]),
            inc="DG", k=16)
        S.emit("gpsimd", lambda e: e.dma_start(
            out=ksel[0:64, 0:144].rearrange("p (j k) -> p j k", k=18)[:, :, 0:1],
            in_=kv_co[0:64, 0:1032].rearrange("p (j k) -> p j k", k=129)[:, :, 128:129],
            allow_slow_non_contiguous=True), inc="DG", k=16)
        S.emit("gpsimd", lambda e: e.dma_start(
            out=ksel[64:128, 1:145].rearrange("p (j k) -> p j k", k=18)[:, :, 0:1],
            in_=kv_co[64:128, 0:1032].rearrange("p (j k) -> p j k", k=129)[:, :, 128:129],
            allow_slow_non_contiguous=True), inc="DG", k=16)
        dg_back = S.count("DG")
        if DEBUG:
            S.wait("sync", "V", v_kvstage)
            S.emit("sync", lambda e: e.dma_start(out=dbg["kvstage"], in_=big0[:, 0:1032]), inc="DO", k=16)
            S.wait("sync", "DG", dg_back)
            S.emit("sync", lambda e: e.dma_start(out=dbg["kvblk"], in_=kvblk[:]), inc="DO", k=16)
            S.emit("sync", lambda e: e.dma_start(out=dbg["ksel"], in_=ksel[:]), inc="DO", k=16)
            S.wait("vector", "DO")

        S.wait("gpsimd", "T", t_ph1)
        S.emit("gpsimd", lambda e: e.dma_start(out=borow[0:1, 0:D], in_=bo), inc="DG", k=16)
        S.emit("gpsimd", lambda e: e.dma_start(
            out=wA[:, 0:8192].rearrange("p (c j) -> p c j", j=D),
            in_=wo.rearrange("(c p) j -> p c j", p=128)), inc="DG", k=16)
        dg_wo = S.count("DG")

        S.wait("tensor", "DG", dg_w0)
        c_qevac = [0, 0, 0, 0]
        t_qfmm = [0, 0, 0, 0]
        t_q_last = 0
        for c in range(8):
            wqbuf = wA if c < 4 else wB
            for tl in range(4):
                wait_bank(tl)
                if c_qevac[tl]:
                    S.wait("tensor", "C", c_qevac[tl])
                for dc in range(8):
                    S.emit("tensor", lambda e, c=c, tl=tl, dc=dc, wqbuf=wqbuf: e.matmul(
                        P[tl][:], wqbuf[:, 8192 + dc * 512 + (c % 4) * 128: 8192 + dc * 512 + (c % 4 + 1) * 128],
                        xfm[:, dc * TC + tl * 512: dc * TC + (tl + 1) * 512],
                        start=(dc == 0), stop=(dc == 7)),
                        inc=("T" if dc == 7 else None))
            t_q = S.count("T")
            t_q_last = t_q
            S.wait("scalar", "T", t_q)
            for tl in range(4):
                if t_qfmm[tl]:
                    S.wait("scalar", "T", t_qfmm[tl])
                S.emit("scalar", lambda e, tl=tl: e.copy(
                    out=xcast[:, tl * 512:(tl + 1) * 512], in_=P[tl][:]), inc="C")
                c_qevac[tl] = S.count("C")
                lastC[tl] = c_qevac[tl]
            for tl in range(4):
                bank = 4 + (tl % 2)
                wait_bank(bank)
                S.wait("tensor", "C", c_qevac[tl])
                S.emit("tensor", lambda e, tl=tl, bank=bank: e.matmul(
                    P[bank][:], projblk[:], xcast[:, tl * 512:(tl + 1) * 512],
                    start=True, stop=True), inc="T")
                t_qfmm[tl] = S.count("T")
                S.wait("vector", "T", t_qfmm[tl])
                if c == 0 and tl == 0:
                    S.wait("vector", "DG", dg_ci)
                S.emit("vector", lambda e, c=c, tl=tl, bank=bank: e.tensor_scalar(
                    out=big0[:, c * TC + tl * 512: c * TC + (tl + 1) * 512],
                    in0=P[bank][:], scalar1=EPS_A, scalar2=EPS_A,
                    op0=OP.add, op1=OP.max), inc="V")
                lastV[bank] = S.count("V")
        v_qf_done = S.count("V")

        S.wait("tensor", "V", v_qf_done)
        S.wait("tensor", "DG", dg_back)
        for tl in range(4):
            wait_bank(6)
            for pr in range(8):
                S.emit("tensor", lambda e, pr=pr, tl=tl: e.matmul(
                    p6[0:16, :], ksel[:, pr * 16:(pr + 1) * 16],
                    big0[:, pr * TC + tl * 512: pr * TC + (tl + 1) * 512],
                    start=(pr == 0), stop=(pr == 7)),
                    inc=("T" if pr == 7 else None))
            t_den = S.count("T")
            S.wait("vector", "T", t_den)
            S.emit("vector", lambda e: e.tensor_scalar(
                out=p5[0:16, :], in0=p6[0:16, :], scalar1=EPS_A, scalar2=None, op0=OP.add))
            S.drain("vector")
            S.emit("vector", lambda e, tl=tl: e.reciprocal(
                out=rden[:, tl * 512:(tl + 1) * 512], in_=p5[0:16, :]), inc="V")
            lastV[6] = S.count("V")

        v_rden = S.count("V")
        S.wait("tensor", "V", v_rden)
        for c in range(8):
            for tl in range(4):
                bn = tl % 2
                bb = 2 + (tl % 2)
                wait_bank(bn)
                wait_bank(bb)
                S.emit("tensor", lambda e, c=c, tl=tl, bn=bn: e.matmul(
                    P[bn][:], kvblk[:, c * 128:(c + 1) * 128],
                    big0[:, c * TC + tl * 512: c * TC + (tl + 1) * 512],
                    start=True, stop=True))
                S.emit("tensor", lambda e, c=c, tl=tl, bb=bb: e.matmul(
                    P[bb][0:64, :], oneh[:, (2 * c) * 128:(2 * c) * 128 + 64],
                    rden[:, tl * 512:(tl + 1) * 512],
                    start=True, stop=True))
                S.emit("tensor", lambda e, c=c, tl=tl, bb=bb: e.matmul(
                    P[bb][64:128, :], oneh[:, (2 * c + 1) * 128 + 64:(2 * c + 1) * 128 + 128],
                    rden[:, tl * 512:(tl + 1) * 512],
                    start=True, stop=True), inc="T")
                t_nb = S.count("T")
                S.wait("vector", "T", t_nb)
                S.emit("vector", lambda e, tl=tl, bb=bb: e.tensor_copy(
                    out=kfm[:, (tl % 2) * 512:(tl % 2 + 1) * 512], in_=P[bb][:]))
                S.emit("vector", lambda e, c=c, tl=tl, bn=bn: e.tensor_tensor(
                    out=big0[:, c * TC + tl * 512: c * TC + (tl + 1) * 512],
                    in0=P[bn][:], in1=kfm[:, (tl % 2) * 512:(tl % 2 + 1) * 512],
                    op=OP.mult), inc="V")
                lastV[bn] = S.count("V")
                lastV[bb] = S.count("V")
        v_outfm = S.count("V")
        if DEBUG:
            S.wait("sync", "V", v_outfm)
            S.emit("sync", lambda e: e.dma_start(out=dbg["outfm"], in_=big0[:]), inc="DO", k=16)
            S.wait("vector", "DO")

        S.wait("tensor", "DG", dg_wo)
        S.wait("tensor", "V", v_outfm)
        S.wait("vector", "DS", 64)
        S.wait("scalar", "DS", 64)
        t_attn_last = 0
        for i in range(NCH):
            b0, b1 = (0, 1) if i % 2 == 0 else (2, 3)
            wait_bank(b0)
            wait_bank(b1)
            for ic in range(8):
                for half, b in ((0, b0), (1, b1)):
                    S.emit("tensor", lambda e, i=i, ic=ic, half=half, b=b: e.matmul(
                        P[b][:], big0[:, ic * TC + i * 128: ic * TC + (i + 1) * 128],
                        wA[:, ic * 1024 + half * 512: ic * 1024 + (half + 1) * 512],
                        start=(ic == 0), stop=False))
            for half, b in ((0, b0), (1, b1)):
                S.emit("tensor", lambda e, half=half, b=b: e.matmul(
                    P[b][:], oneh[0:1, 0:128], borow[0:1, half * 512:(half + 1) * 512],
                    start=False, stop=True), inc=("T" if half == 1 else None))
            t_attn_last = S.count("T")
            S.wait("vector", "T", t_attn_last)
            for half, b in ((0, b0), (1, b1)):
                st_dst = stA if half == 0 else stT
                S.emit("vector", lambda e, i=i, half=half, b=b, st_dst=st_dst: e.scalar_tensor_tensor(
                    out=resid[:, i * 1024 + half * 512: i * 1024 + half * 512 + 512],
                    in0=P[b][:], scalar=1.0,
                    in1=resid[:, i * 1024 + half * 512: i * 1024 + half * 512 + 512],
                    op0=OP.mult, op1=OP.add,
                    accum_out=st_dst[:, i:i + 1]), inc="V")
            lastV[b0] = S.count("V")
            lastV[b1] = S.count("V")
            v_res = S.count("V")
            S.wait("scalar", "V", v_res)
            S.emit("scalar", lambda e, i=i: e.activation(
                out=p4[:], in_=resid[:, i * 1024: i * 1024 + 512], func=AF.Square,
                accum_out=stB[:, i:i + 1]))
            S.emit("scalar", lambda e, i=i: e.activation(
                out=p5[:], in_=resid[:, i * 1024 + 512:(i + 1) * 1024], func=AF.Square,
                accum_out=stC[:, i:i + 1]), inc="C")
        c_sq1 = S.count("C")
        lastC[4] = c_sq1
        lastC[5] = c_sq1

        def ln_smalls():
            S.emit("vector", lambda e: e.tensor_tensor(out=stU[:], in0=stA[:], in1=stT[:], op=OP.add))
            S.drain("vector")
            S.emit("vector", lambda e: e.tensor_scalar(out=stM[:], in0=stU[:], scalar1=1.0 / D,
                                                       scalar2=None, op0=OP.mult))
            S.emit("vector", lambda e: e.tensor_tensor(out=stU[:], in0=stB[:], in1=stC[:], op=OP.add))
            S.drain("vector")
            S.emit("vector", lambda e: e.tensor_scalar(out=stU[:], in0=stU[:], scalar1=1.0 / D,
                                                       scalar2=None, op0=OP.mult))
            S.drain("vector")
            S.emit("vector", lambda e: e.tensor_tensor(out=stV2[:], in0=stM[:], in1=stM[:], op=OP.mult))
            S.drain("vector")
            S.emit("vector", lambda e: e.tensor_tensor(out=stU[:], in0=stU[:], in1=stV2[:], op=OP.subtract))
            S.drain("vector")
            S.emit("vector", lambda e: e.tensor_scalar(out=stU[:], in0=stU[:], scalar1=EPS_LN,
                                                       scalar2=None, op0=OP.add), inc="V")
            v_var = S.count("V")
            S.wait("scalar", "V", v_var)
            S.emit("scalar", lambda e: e.activation(out=stV2[:], in_=stU[:], func=AF.Sqrt), inc="C")
            c_sqrt = S.count("C")
            S.wait("vector", "C", c_sqrt)
            S.emit("vector", lambda e: e.reciprocal(out=stR[:], in_=stV2[:]))
            S.drain("vector")
            S.emit("vector", lambda e: e.tensor_tensor(out=stT[:], in0=stR[:], in1=stR[:], op=OP.mult))
            S.drain("vector")
            S.emit("vector", lambda e: e.tensor_tensor(out=stT[:], in0=stT[:], in1=stU[:], op=OP.mult))
            S.drain("vector")
            S.emit("vector", lambda e: e.tensor_scalar(out=stT[:], in0=stT[:], scalar1=-0.5,
                                                       scalar2=1.5, op0=OP.mult, op1=OP.add))
            S.drain("vector")
            S.emit("vector", lambda e: e.tensor_tensor(out=stR[:], in0=stR[:], in1=stT[:], op=OP.mult),
                   inc="V")
            S.drain("vector")

        S.wait("vector", "C", c_sq1)
        ln_smalls()

        t_x1tr = [0] * NCH
        v_x1_chunk = [0] * NCH
        c_apply = [0] * NCH
        S.emit("vector", lambda e: e.scalar_tensor_tensor(
            out=stT[:], in0=stM[:], scalar=-1.0, op0=OP.mult,
            in1=stR[:], op1=OP.mult), inc="V")
        v_lnp = S.count("V")
        S.wait("scalar", "V", v_lnp)
        for i in range(NCH):
            S.emit("scalar", lambda e, i=i: e.activation(
                out=resid[:, i * 1024:(i + 1) * 1024],
                in_=resid[:, i * 1024:(i + 1) * 1024],
                func=AF.Identity, bias=stT[:, i:i + 1],
                scale=stR[:, i:i + 1]), inc="C")
            c_apply[i] = S.count("C")
        for i in range(NCH):
            S.wait("tensor", "C", c_apply[i])
            for half in range(2):
                if v_pT:
                    S.wait("tensor", "V", v_pT)
                for c4 in range(4):
                    c = half * 4 + c4
                    S.emit("tensor", lambda e, i=i, c=c, c4=c4: e.matmul(
                        pT[:, c4 * 128:(c4 + 1) * 128],
                        resid[:, i * 1024 + c * 128:i * 1024 + (c + 1) * 128],
                        identf, is_transpose=True, start=(c4 == 0), stop=(c4 == 3)),
                        inc=("T" if c4 == 3 else None))
                tb = S.count("T")
                S.wait("vector", "T", tb)
                S.emit("vector", lambda e, i=i, half=half: e.tensor_copy(
                    out=xfm[:].rearrange("p (c t) -> p c t", t=TC)
                        [:, half * 4:(half + 1) * 4, i * 128:(i + 1) * 128],
                    in_=pT[:].rearrange("p (c t) -> p c t", t=128)), inc="V")
                v_pT = S.count("V")
            t_x1tr[i] = S.count("T")
            v_x1_chunk[i] = S.count("V")
        v_x1fm = S.count("V")
        if DEBUG:
            S.wait("sync", "V", v_x1fm)
            S.emit("sync", lambda e: e.dma_start(
                out=dbg["x1"].rearrange("(i p) d -> p i d", p=128),
                in_=resid[:].rearrange("p (i d) -> p i d", d=D)), inc="DO", k=16)
            S.wait("vector", "DO")

        for tl in range(4):
            S.wait("tensor", "V", v_x1_chunk[4 * tl + 3])
            wait_bank(6)
            for dc in range(8):
                S.emit("tensor", lambda e, tl=tl, dc=dc: e.matmul(
                    p6[0:16, :], gwtb[:, dc * 16:(dc + 1) * 16],
                    xfm[:, dc * TC + tl * 512: dc * TC + (tl + 1) * 512],
                    start=(dc == 0), stop=(dc == 7)), inc=("T" if dc == 7 else None))
            t_g = S.count("T")
            S.wait("vector", "T", t_g)
            S.emit("vector", lambda e, tl=tl: e.tensor_copy(
                out=gscb[:, tl * 512:(tl + 1) * 512], in_=p6[0:16, :]), inc="V")
            lastV[6] = S.count("V")
        v_logits = S.count("V")

        S.wait("tensor", "V", v_logits)
        c_scgrp = 0
        for g in range(2):
            if v_pT:
                S.wait("tensor", "V", v_pT)
            if c_scgrp:
                S.wait("tensor", "C", c_scgrp)
            for q in range(8):
                i = g * 8 + q
                S.emit("tensor", lambda e, i=i, q=q: e.matmul(
                    pT[:, q * 16:(q + 1) * 16],
                    gscb[:, i * 128:(i + 1) * 128],
                    ident[0:16, 0:16],
                    start=(q == 0), stop=(q == 7)), inc=("T" if q == 7 else None))
            t_sc = S.count("T")
            S.wait("scalar", "T", t_sc)
            S.emit("scalar", lambda e, g=g: e.activation(
                out=scorebuf[:, g * 128:(g + 1) * 128], in_=pT[:, 0:128],
                func=AF.Sigmoid), inc="C")
            c_scgrp = S.count("C")
        c_scores = S.count("C")
        if DEBUG:
            S.wait("sync", "C", c_scores)
            S.emit("sync", lambda e: e.dma_start(out=dbg["scores"], in_=scorebuf[:]), inc="DO", k=16)

        S.wait("vector", "C", c_scores)
        S.wait("vector", "DS", 64)
        LANES = [
            dict(t=rt_t, g=rt_g, g2=rt_g2, g8=rt_g8, m8=rt_m8, keep=rt_keep,
                 sm=rt_sm, sel=rt_sel),
            dict(t=rt2_t, g=rt2_g, g2=rt2_g2, g8=rt2_g8, m8=rt2_m8, keep=rt2_keep,
                 sm=rt2_sm, sel=rt2_sel),
        ]

        def r_steps(i, L):
            sc = scorebuf[:, i * 16:(i + 1) * 16]
            yield lambda e: e.tensor_tensor(out=L["t"][:], in0=sc, in1=gbtile[:], op=OP.add)
            yield lambda e: e.tensor_tensor(out=L["g"][:], in0=L["t"][:, 0:16:4],
                                            in1=L["t"][:, 1:16:4], op=OP.max)
            yield lambda e: e.tensor_tensor(out=L["g2"][:], in0=L["t"][:, 2:16:4],
                                            in1=L["t"][:, 3:16:4], op=OP.max)
            yield lambda e: e.tensor_tensor(out=L["g8"][:, 0:4], in0=L["g"][:],
                                            in1=L["g2"][:], op=OP.max)
            yield lambda e: e.max(out=L["m8"][:], in_=L["g8"][:])
            yield lambda e: e.tensor_scalar(out=L["g"][:], in0=L["g8"][:, 0:4],
                                            scalar1=L["m8"][:, 1:2], scalar2=None, op0=OP.is_ge)
            yield lambda e: e.tensor_copy(out=L["keep"][:, 0:16:4], in_=L["g"][:])
            yield lambda e: e.tensor_copy(out=L["keep"][:, 1:16:4], in_=L["g"][:])
            yield lambda e: e.tensor_copy(out=L["keep"][:, 2:16:4], in_=L["g"][:])
            yield lambda e: e.tensor_copy(out=L["keep"][:, 3:16:4], in_=L["g"][:])
            yield lambda e: e.tensor_scalar(out=L["sm"][:], in0=L["keep"][:], scalar1=BIG,
                                            scalar2=BIG, op0=OP.mult, op1=OP.subtract)
            yield lambda e: e.tensor_tensor(out=L["sm"][:], in0=L["t"][:], in1=L["sm"][:], op=OP.add)
            yield lambda e: e.max(out=L["m8"][:], in_=L["sm"][:])
            yield lambda e: e.tensor_scalar(out=L["sel"][:], in0=L["sm"][:],
                                            scalar1=L["m8"][:, 3:4], scalar2=None, op0=OP.is_ge)
            yield lambda e, i=i: e.tensor_tensor(out=combtm[:, i * 16:(i + 1) * 16],
                                                 in0=L["sel"][:], in1=sc, op=OP.mult)

        v_rpair = [0] * 8
        for pair in range(8):
            i0, i1 = 2 * pair, 2 * pair + 1
            gens = [r_steps(i0, LANES[0]), r_steps(i1, LANES[1])]
            ops0 = list(gens[0]); ops1 = list(gens[1])
            for k in range(len(ops0)):
                S.emit("vector", ops0[k])
                S.emit("vector", ops1[k],
                       inc=("V" if k == len(ops0) - 1 else None))
            v_rpair[pair] = S.count("V")
        v_comb = S.count("V")
        if DEBUG:
            S.wait("sync", "V", v_comb)
            S.emit("sync", lambda e: e.dma_start(out=dbg["comb"], in_=combtm[:]), inc="DO", k=16)

        S.wait("tensor", "C", c_scores)
        for g in range(4):
            S.wait("tensor", "V", v_rpair[2 * g + 1])
            if v_pT:
                S.wait("tensor", "V", v_pT)
            for q in range(4):
                i = g * 4 + q
                S.emit("tensor", lambda e, i=i, q=q: e.matmul(
                    pT[0:16, q * 128:(q + 1) * 128],
                    combtm[:, i * 16:(i + 1) * 16],
                    ident[:],
                    start=(q == 0), stop=(q == 3)), inc=("T" if q == 3 else None))
            t_cf = S.count("T")
            S.wait("vector", "T", t_cf)
            S.emit("vector", lambda e, g=g: e.tensor_copy(
                out=combfm[:, g * 512:(g + 1) * 512], in_=pT[0:16, 0:512]), inc="V")
            v_pT = S.count("V")
        v_combfm = S.count("V")

        S.wait("vector", "C", c_apply[NCH - 1])
        SLOT = {
            "A0": wA[:, 0:4096], "A1": wA[:, 4096:8192], "A2": wA[:, 8192:12288],
            "B0": wB[:, 0:4096], "B1": wB[:, 4096:8192], "B2": wB[:, 8192:12288],
        }
        TRIPLE = {0: ("B0", "B1", "A2"), 1: ("B2", "A0", "A1")}
        slot_free = {
            "B0": t_ph1, "B1": t_ph1,
            "A2": t_q_last, "B2": t_q_last,
            "A0": t_attn_last, "A1": t_attn_last,
        }
        dg13 = [0] * 17
        dg2 = [0] * 17
        t_h3_done = [0] * 17

        def srcs_of(idx):
            if idx < 16:
                return w1t[idx], w3t[idx], w2t[idx]
            return ws1t, ws3t, ws2t

        def issue_dma13(idx):
            s1n, s3n, _ = TRIPLE[idx % 2]
            s1, s3, _ = srcs_of(idx)
            for sl, src in ((s1n, s1), (s3n, s3)):
                if slot_free[sl]:
                    S.wait("gpsimd", "T", slot_free[sl])
                S.emit("gpsimd", lambda e, dst=SLOT[sl], src=src: e.dma_start(
                    out=dst.rearrange("p (c j) -> p c j", j=INTER),
                    in_=src.rearrange("(c p) j -> p c j", p=128)), inc="DG", k=16)
            dg13[idx] = S.count("DG")

        def issue_dma2(idx):
            _, _, s2n = TRIPLE[idx % 2]
            _, _, s2 = srcs_of(idx)
            if slot_free[s2n]:
                S.wait("gpsimd", "T", slot_free[s2n])
            S.emit("gpsimd", lambda e, dst=SLOT[s2n], src=s2: e.dma_start(
                out=dst.rearrange("p (c j) -> p c j", j=D),
                in_=src.rearrange("(c p) j -> p c j", p=128)), inc="DG", k=16)
            dg2[idx] = S.count("DG")

        issue_dma13(0)
        issue_dma2(0)
        issue_dma13(1)
        issue_dma2(1)
        lastVkfm = [0]

        def ffn_expert(idx, slot):
            s1n, s3n, _ = TRIPLE[idx % 2]
            w1s, w3s = SLOT[s1n], SLOT[s3n]
            hbase = slot * 8192
            if slot == 0:
                cbp = [kfm[:, 2048 + t * 512:2048 + (t + 1) * 512] for t in range(4)]
            else:
                cbp = [kvblk[:, 0:512], kvblk[:, 512:1024], kftm[:, 0:512], kftm[:, 512:1024]]
            shared = (idx == 16)
            S.wait("tensor", "DG", dg13[idx])
            if not shared:
                for tl, b in ((0, 4), (1, 5), (2, 6)):
                    wait_bank(b)
                    S.emit("tensor", lambda e, idx=idx, tl=tl, b=b: e.matmul(
                        P[b][:], oneh[:, idx * 128:(idx + 1) * 128],
                        combfm[:, tl * 512:(tl + 1) * 512],
                        start=True, stop=True), inc=("T" if tl == 2 else None))
                t_cb1 = S.count("T")
                S.wait("vector", "T", t_cb1)
                v_cb0 = 0
                for tl, b in ((0, 4), (1, 5), (2, 6)):
                    S.emit("vector", lambda e, cb=cbp[tl], b=b: e.tensor_copy(
                        out=cb, in_=P[b][:]), inc="V")
                    lastV[b] = S.count("V")
                    if tl == 0:
                        v_cb0 = S.count("V")
                S.wait("tensor", "V", v_cb0)
                S.emit("tensor", lambda e, idx=idx: e.matmul(
                    P[4][:], oneh[:, idx * 128:(idx + 1) * 128],
                    combfm[:, 3 * 512:4 * 512],
                    start=True, stop=True), inc="T")
                t_cb2 = S.count("T")
                S.wait("vector", "T", t_cb2)
                S.emit("vector", lambda e, cb=cbp[3]: e.tensor_copy(
                    out=cb, in_=P[4][:]), inc="V")
                lastV[4] = S.count("V")
            TO = [1, 2, 3, 0]
            H3B = [4, 5, 6, 0]
            for ic in range(4):
                t_h1t = {}
                for tl in TO:
                    wait_bank(tl)
                    for dc in range(8):
                        S.emit("tensor", lambda e, w1s=w1s, ic=ic, tl=tl, dc=dc: e.matmul(
                            P[tl][:], w1s[:, dc * 512 + ic * 128: dc * 512 + (ic + 1) * 128],
                            xfm[:, dc * TC + tl * 512: dc * TC + (tl + 1) * 512],
                            start=(dc == 0), stop=(dc == 7)),
                            inc=("T" if dc == 7 else None))
                    t_h1t[tl] = S.count("T")
                if lastVkfm[0]:
                    S.wait("scalar", "V", lastVkfm[0])
                c_silut = {}
                for tl in TO:
                    S.wait("scalar", "T", t_h1t[tl])
                    S.emit("scalar", lambda e, tl=tl: e.activation(
                        out=kfm[:, tl * 512:(tl + 1) * 512], in_=P[tl][:], func=AF.Silu),
                        inc="C")
                    c_silut[tl] = S.count("C")
                    lastC[tl] = c_silut[tl]
                t_h3t = {}
                for k, tl in enumerate(TO):
                    b = H3B[k]
                    if b == 0:
                        S.wait("tensor", "C", c_silut[0])
                    wait_bank(b)
                    for dc in range(8):
                        S.emit("tensor", lambda e, w3s=w3s, ic=ic, tl=tl, dc=dc, b=b: e.matmul(
                            P[b][:], w3s[:, dc * 512 + ic * 128: dc * 512 + (ic + 1) * 128],
                            xfm[:, dc * TC + tl * 512: dc * TC + (tl + 1) * 512],
                            start=(dc == 0), stop=(dc == 7)),
                            inc=("T" if dc == 7 else None))
                    t_h3t[tl] = S.count("T")
                for k, tl in enumerate(TO):
                    b = H3B[k]
                    o = hbase + ic * TC + tl * 512
                    S.wait("vector", "T", t_h3t[tl])
                    S.wait("vector", "C", c_silut[tl])
                    S.emit("vector", lambda e, tl=tl, o=o, b=b: e.tensor_tensor(
                        out=big0[:, o:o + 512], in0=P[b][:],
                        in1=kfm[:, tl * 512:(tl + 1) * 512], op=OP.mult), inc="V")
                    lastV[b] = S.count("V")
                lastVkfm[0] = S.count("V")
                if not shared:
                    for tl in TO:
                        o = hbase + ic * TC + tl * 512
                        S.emit("vector", lambda e, tl=tl, o=o, cb=cbp[tl]: e.tensor_tensor(
                            out=big0[:, o:o + 512], in0=big0[:, o:o + 512],
                            in1=cb, op=OP.mult),
                            inc=("V" if tl == TO[-1] else None))
            t_h3_done[idx] = S.count("T")
            slot_free[s1n] = t_h3_done[idx]
            slot_free[s3n] = t_h3_done[idx]

        YB = [4, 5, 6, 0, 1, 2, 3]

        def ffn_w2(members, v_ready, fuse_ln2=False):
            nmem = len(members)
            S.wait("tensor", "V", v_ready)
            for idx, _ in members:
                S.wait("tensor", "DG", dg2[idx])
            for i in range(NCH):
                b0 = YB[(2 * i) % 7]
                b1 = YB[(2 * i + 1) % 7]
                wait_bank(b0)
                wait_bank(b1)
                for mi, (idx, slot) in enumerate(members):
                    w2s = SLOT[TRIPLE[idx % 2][2]]
                    hbase = slot * 8192
                    for ic in range(4):
                        for half, b in ((0, b0), (1, b1)):
                            first = (mi == 0 and ic == 0)
                            last = (mi == nmem - 1 and ic == 3)
                            S.emit("tensor", lambda e, w2s=w2s, hbase=hbase, i=i, ic=ic,
                                   half=half, b=b, first=first, last=last: e.matmul(
                                P[b][:],
                                big0[:, hbase + ic * TC + i * 128: hbase + ic * TC + (i + 1) * 128],
                                w2s[:, ic * 1024 + half * 512: ic * 1024 + (half + 1) * 512],
                                start=first, stop=last),
                                inc=("T" if (last and half == 1) else None))
                t_y = S.count("T")
                S.wait("vector", "T", t_y)
                for half, b in ((0, b0), (1, b1)):
                    if fuse_ln2:
                        st_dst = stA if half == 0 else stT
                        S.emit("vector", lambda e, i=i, half=half, b=b, st_dst=st_dst: e.scalar_tensor_tensor(
                            out=resid[:, i * 1024 + half * 512: i * 1024 + half * 512 + 512],
                            in0=P[b][:], scalar=1.0,
                            in1=resid[:, i * 1024 + half * 512: i * 1024 + half * 512 + 512],
                            op0=OP.mult, op1=OP.add,
                            accum_out=st_dst[:, i:i + 1]), inc=("V" if half == 1 else None))
                    else:
                        S.emit("vector", lambda e, i=i, half=half, b=b: e.tensor_tensor(
                            out=resid[:, i * 1024 + half * 512: i * 1024 + half * 512 + 512],
                            in0=P[b][:],
                            in1=resid[:, i * 1024 + half * 512: i * 1024 + half * 512 + 512],
                            op=OP.add), inc=("V" if half == 1 else None))
                vv = S.count("V")
                lastV[b0] = vv
                lastV[b1] = vv
                if fuse_ln2:
                    sq0 = YB[(2 * i + 4) % 7]
                    sq1 = YB[(2 * i + 5) % 7]
                    S.wait("scalar", "V", vv)
                    S.emit("scalar", lambda e, i=i, sq0=sq0: e.activation(
                        out=P[sq0][:], in_=resid[:, i * 1024: i * 1024 + 512], func=AF.Square,
                        accum_out=stB[:, i:i + 1]))
                    S.emit("scalar", lambda e, i=i, sq1=sq1: e.activation(
                        out=P[sq1][:], in_=resid[:, i * 1024 + 512:(i + 1) * 1024], func=AF.Square,
                        accum_out=stC[:, i:i + 1]), inc="C")
                    lastC[sq0] = S.count("C")
                    lastC[sq1] = S.count("C")
            t_w2 = S.count("T")
            for idx, _ in members:
                slot_free[TRIPLE[idx % 2][2]] = t_w2
            return t_w2

        S.wait("tensor", "V", v_combfm)
        for pi in range(8):
            e0, e1 = 2 * pi, 2 * pi + 1
            ffn_expert(e0, 0)
            ffn_expert(e1, 1)
            if pi < 7:
                issue_dma13(2 * (pi + 1))
                issue_dma13(2 * (pi + 1) + 1)
            elif pi == 7:
                issue_dma13(16)
            t_w2 = ffn_w2([(e0, 0), (e1, 1)], S.count("V"))
            if pi < 7:
                issue_dma2(2 * (pi + 1))
                issue_dma2(2 * (pi + 1) + 1)
            else:
                issue_dma2(16)
        ffn_expert(16, 0)
        ffn_w2([(16, 0)], S.count("V"), fuse_ln2=True)
        v_r2 = S.count("V")
        c_ln2sq = S.count("C")

        S.wait("vector", "C", c_ln2sq)
        ln_smalls()
        S.emit("vector", lambda e: e.scalar_tensor_tensor(
            out=stT[:], in0=stM[:], scalar=-1.0, op0=OP.mult,
            in1=stR[:], op1=OP.mult), inc="V")
        S.wait("scalar", "V", S.count("V"))
        for i in range(NCH):
            if i % 2 == 0:
                S.emit("vector", lambda e, i=i: e.tensor_scalar(
                    out=resid[:, i * 1024:(i + 1) * 1024],
                    in0=resid[:, i * 1024:(i + 1) * 1024],
                    scalar1=stM[:, i:i + 1], scalar2=stR[:, i:i + 1],
                    op0=OP.subtract, op1=OP.mult), inc=("V" if i % 4 == 2 else None))
            else:
                S.emit("scalar", lambda e, i=i: e.activation(
                    out=resid[:, i * 1024:(i + 1) * 1024],
                    in_=resid[:, i * 1024:(i + 1) * 1024],
                    func=AF.Identity, bias=stT[:, i:i + 1],
                    scale=stR[:, i:i + 1]), inc=("C" if i % 4 == 3 else None))
            if i % 4 == 3:
                g = i // 4
                S.wait("sync", "V", S.count("V"))
                S.wait("sync", "C", S.count("C"))
                S.emit("sync", lambda e, g=g: e.dma_start(
                    out=out[g * 512:(g + 1) * 512, :].rearrange("(i p) d -> p i d", p=128),
                    in_=resid[:, g * 4096:(g + 1) * 4096].rearrange("p (i d) -> p i d", d=D)),
                    inc="DO", k=16)
        S.wait("sync", "DO")
        with nc.allow_low_precision(reason="bf16 transposes batched into one PSUM bank write disjoint column slots; no cross-slot accumulation occurs"):
            S.run(block)
    return nc


def kernel(**inputs):
    x = np.asarray(inputs["L0_attn_x"], dtype=np.float32).reshape(4 * 4096, D)
    f = np.float32
    wqv = np.ascontiguousarray(inputs["L0_attn_Wq"], dtype=f)
    wkv = np.ascontiguousarray(inputs["L0_attn_Wk"], dtype=f)
    wvv = np.ascontiguousarray(inputs["L0_attn_Wv"], dtype=f)
    proj = np.ascontiguousarray(inputs["L0_attn_proj"], dtype=f)
    wov = np.ascontiguousarray(inputs["L0_attn_Wo"], dtype=f)
    bov = np.ascontiguousarray(inputs["L0_attn_bo"], dtype=f).reshape(1, D)
    gwtv = np.ascontiguousarray(np.asarray(inputs["L0_ffn_gate_w"], dtype=f).T)
    gbv = np.ascontiguousarray(
        np.broadcast_to(np.asarray(inputs["L0_ffn_gate_bias"], dtype=f).reshape(1, E), (128, E)))
    w1tv = np.ascontiguousarray(np.asarray(inputs["L0_ffn_w1"], dtype=f).transpose(0, 2, 1))
    w3tv = np.ascontiguousarray(np.asarray(inputs["L0_ffn_w3"], dtype=f).transpose(0, 2, 1))
    w2tv = np.ascontiguousarray(np.asarray(inputs["L0_ffn_w2"], dtype=f).transpose(0, 2, 1))
    ws1tv = np.ascontiguousarray(np.asarray(inputs["L0_ffn_ws1"], dtype=f).T)
    ws3tv = np.ascontiguousarray(np.asarray(inputs["L0_ffn_ws3"], dtype=f).T)
    ws2tv = np.ascontiguousarray(np.asarray(inputs["L0_ffn_ws2"], dtype=f).T)

    identv = np.eye(128, dtype=ml_dtypes.bfloat16)
    onehv = np.zeros((16, 2048), dtype=ml_dtypes.bfloat16)
    for e_ in range(16):
        onehv[e_, e_ * 128:(e_ + 1) * 128] = 1
    nc = build_nc()
    in_maps = []
    for c in range(N_CORES):
        b, h = c // 2, c % 2
        own = np.ascontiguousarray(x[b * 4096 + h * TC: b * 4096 + (h + 1) * TC])
        in_maps.append({
            "x_own": own,
            "wq": wqv, "wk": wkv, "wv": wvv, "proj": proj, "wo": wov, "bo": bov,
            "gwt": gwtv, "gbias": gbv,
            "w1t": w1tv, "w3t": w3tv, "w2t": w2tv,
            "ws1t": ws1tv, "ws3t": ws3tv, "ws2t": ws2tv,
            "identc": identv, "onehc": onehv,
        })
    res = run_bass_kernel_spmd(nc, in_maps, core_ids=list(range(N_CORES)))
    outs = [np.asarray(res.results[c]["out"]) for c in range(N_CORES)]
    return np.concatenate(outs, axis=0).reshape(4, 4096, D).astype(np.float32)
